# version 20
# baseline (speedup 1.0000x reference)
import numpy as np
import ml_dtypes
from contextlib import ExitStack

import concourse.bass as bass
import concourse.mybir as mybir
from concourse.alu_op_type import AluOpType as OP
from concourse.bass_utils import run_bass_kernel_spmd

F32 = mybir.dt.float32
BF16 = mybir.dt.bfloat16
I32 = mybir.dt.int32
ACT = mybir.ActivationFunctionType
AX = mybir.AxisListType

S = 2048
D = 2048
HD = 64
QSCALE = HD ** -0.5
EXPB = 10.0
NT = S // 128
NF = D // 128
TINY = 1e-30
INV2PI = float(1.0 / (2.0 * np.pi))
TWO_PI = float(2.0 * np.pi)
PI_CL = 3.1415925
LN1E4_32 = float(np.log(10000.0) / 32.0)


class Plan:
    def __init__(self):
        self.ops = {k: [] for k in ("sync", "tensor", "vector", "scalar", "gpsimd")}
        self.cnt = {}

    def emit(self, eng, fn, waits=(), inc=None):
        waits = tuple((s, v) for (s, v) in waits if v > 0)

        def op(e, _fn=fn, _w=waits, _i=inc):
            for sem, val in _w:
                e.wait_ge(sem, val)
            r = _fn(e)
            if _i is not None:
                r.then_inc(_i[0], _i[1])
        self.ops[eng].append(op)
        if inc is not None:
            self.cnt[id(inc[0])] = self.cnt.get(id(inc[0]), 0) + inc[1]

    def c(self, sem):
        return self.cnt.get(id(sem), 0)


def build_nc():
    nc = bass.Bass("TRN2", target_bir_lowering=False)
    x_d = nc.dram_tensor("x", [S, D], F32, kind="ExternalInput").ap()
    kva_d = nc.dram_tensor("kv_a_w", [D, 512], F32, kind="ExternalInput").ap()
    kvb_d = nc.dram_tensor("kv_b_w", [512, 512], F32, kind="ExternalInput").ap()
    ow_d = nc.dram_tensor("o_w", [D, D], F32, kind="ExternalInput").ap()
    fk_d = nc.dram_tensor("fk", [D, D], F32, kind="ExternalInput").ap()
    fb_d = nc.dram_tensor("fb", [1, D], F32, kind="ExternalInput").ap()
    id_d = nc.dram_tensor("ident", [128, 128], BF16, kind="ExternalInput").ap()
    dg_d = nc.dram_tensor("diag01", [128, 128], BF16, kind="ExternalInput").ap()
    dn_d = nc.dram_tensor("diagneg", [128, 128], BF16, kind="ExternalInput").ap()
    out_d = nc.dram_tensor("out", [S, D], F32, kind="ExternalOutput").ap()
    kq_s = nc.dram_tensor("kq_scr", [D, D], BF16, kind="Internal").ap()
    x1_s = nc.dram_tensor("x1_scr", [S, D], F32, kind="Internal").ap()
    xqt_s = nc.dram_tensor("xqt_scr", [S, D], BF16, kind="Internal").ap()
    bqr_s = nc.dram_tensor("bqr_scr", [1, D], F32, kind="Internal").ap()

    es = ExitStack()
    with es:
        def sbt(name, shape, dt):
            return es.enter_context(nc.sbuf_tensor(name, shape, dt))

        AR1 = sbt("AR1", [128, 32768], BF16)
        AR2 = sbt("AR2", [128, 32768], BF16)
        AR3 = sbt("AR3", [128, 8192], F32)
        AR4 = sbt("AR4", [128, 4096], BF16)
        K2 = sbt("K2", [128, 4 * S], BF16)
        VT = sbt("VT", [128, 4 * NT * 65], BF16)
        EB = sbt("EB", [128, 3072], BF16)
        IDT = sbt("IDT", [128, 128], BF16)
        DG = sbt("DG", [128, 128], BF16)
        DGN = sbt("DGN", [128, 128], BF16)
        ONE = sbt("ONE", [64, 128], F32)
        NR = sbt("NR", [64, 1024], F32)
        UNB = sbt("UNB", [64, 2048], BF16)
        STT = sbt("STT", [128, 16], F32)
        S1B = sbt("S1B", [128, NT], F32)
        NEGB = sbt("NEGB", [128, 1], F32)
        PS = es.enter_context(nc.psum_tensor("PS", [128, 4096], F32))

        sL = es.enter_context(nc.semaphore("sL"))
        sG = es.enter_context(nc.semaphore("sG"))
        sS = es.enter_context(nc.semaphore("sSt"))
        sT = es.enter_context(nc.semaphore("sT"))
        sV = es.enter_context(nc.semaphore("sV"))
        sC = es.enter_context(nc.semaphore("sC"))
        blk = es.enter_context(nc.Block())

        P = Plan()
        AR3i = AR3.bitcast(I32)
        AR3b = AR3.bitcast(BF16)
        EBi = EB.bitcast(I32)
        COSP = AR3b[:, 8192:10240]
        SINP = AR3b[:, 10240:12288]
        HLV = AR4.bitcast(F32)[0:128, 0:2048]
        TRW = AR3[0:32, 0:2048]
        BQ = K2[:, 0:2048]
        K2F = K2.bitcast(F32)

        def bank(i):
            return PS[:, i * 512:(i + 1) * 512]

        def bankb(i):
            return PS.bitcast(BF16)[:, i * 1024:(i + 1) * 1024]

        def vfence():
            P.emit("vector", lambda e: e.tensor_copy(STT[0:1, 15:16], STT[0:1, 15:16]),
                   inc=(sV, 1))
            return P.c(sV)

        def vv(fn2, extra=()):
            P.emit("vector", fn2, waits=[(sV, P.c(sV))] + list(extra), inc=(sV, 1))

        P.emit("sync", lambda e: e.dma_start(out=IDT[:], in_=id_d[:]), inc=(sL, 16))
        P.emit("sync", lambda e: e.dma_start(out=DG[:], in_=dg_d[:]), inc=(sL, 16))
        P.emit("sync", lambda e: e.dma_start(out=DGN[:], in_=dn_d[:]), inc=(sL, 16))
        P.emit("sync", lambda e: e.dma_start(out=AR3[0:1, 0:D], in_=fb_d[:]),
               inc=(sL, 16))
        ld_consts = P.c(sL)
        P.emit("vector", lambda e: e.memset(NEGB[:], -EXPB), inc=(sV, 1))
        P.emit("vector", lambda e: e.memset(VT[:], 1.0), inc=(sV, 1))
        P.emit("vector", lambda e: e.memset(NR[:], 0.0), inc=(sV, 1))
        P.emit("vector", lambda e: e.memset(STT[:], 0.0), inc=(sV, 1))
        P.emit("vector", lambda e: e.memset(STT[:, 14:15], 1e-5), inc=(sV, 1))
        P.emit("vector", lambda e: e.memset(ONE[:], 0.0), inc=(sV, 1))
        P.emit("vector", lambda e: e.memset(ONE[0:1, :], 1.0), inc=(sV, 1))
        P.emit("vector", lambda e: e.memset(ONE[32:33, :], 1.0), inc=(sV, 1))
        v_memset = P.c(sV)

        fbrow = AR3[0:1, 0:D]
        P.emit("vector", lambda e: e.reduce_max(STT[0:1, 8:9], fbrow, axis=AX.X,
                                                apply_absolute_value=True),
               waits=[(sL, ld_consts)], inc=(sV, 1))
        vv(lambda e: e.tensor_scalar(STT[0:1, 8:9], STT[0:1, 8:9],
                                     TINY, None, OP.max))
        vv(lambda e: e.reciprocal(STT[0:1, 9:10], STT[0:1, 8:9]))
        vv(lambda e: e.tensor_scalar(STT[0:1, 9:10], STT[0:1, 9:10],
                                     127.0, None, OP.mult))
        vv(lambda e: e.tensor_scalar(STT[0:1, 10:11], STT[0:1, 8:9],
                                     1.0 / 127.0, None, OP.mult))
        vv(lambda e: e.tensor_scalar(AR3i[0:1, 2048:2048 + D], fbrow,
                                     STT[0:1, 9:10], None, OP.mult))
        vv(lambda e: e.tensor_scalar(AR3[0:1, 2048:2048 + D],
                                     AR3i[0:1, 2048:2048 + D],
                                     STT[0:1, 10:11], None, OP.mult))
        v_bqrow = P.c(sV)
        P.emit("sync", lambda e: e.dma_start(out=bqr_s[:], in_=AR3[0:1, 2048:2048 + D]),
               waits=[(sV, v_bqrow)], inc=(sS, 16))
        st_bqrow = P.c(sS)
        v_consts_done = P.c(sV)

        KF = [AR3[:, 0:2048], AR3[:, 2048:4096]]
        CMX = AR3[:, 4096:6144]
        BCA = AR3[:, 6144:8192]
        AR4i = AR4.bitcast(I32)[:, 0:1024]
        KQST = AR4[:, 2048:4096]
        wq_chunks = []
        wq_state = {"buf_free": [0, 0], "ar4_free": 0, "st_kq": 0}

        def wq_amax(i):
            def dma(first_wait):
                b = i % 2
                P.emit("sync", lambda e, i=i, b=b: e.dma_start(
                    out=KF[b][:], in_=fk_d[i * 128:(i + 1) * 128, :]),
                    waits=[(sV, max(wq_state["buf_free"][b], first_wait))],
                    inc=(sL, 16))
                wq_state["ld" + str(i)] = P.c(sL)

            def vec(first_wait):
                b = i % 2
                wld = wq_state["ld" + str(i)]
                dst = CMX if i == 0 else BCA
                P.emit("vector", lambda e, b=b, dst=dst: e.tensor_scalar(
                    dst.bitcast(I32)[:, 0:2048], KF[b].bitcast(I32)[:, 0:2048],
                    0x7FFFFFFF, None, OP.bitwise_and),
                    waits=[(sL, wld)], inc=(sV, 1))
                if i > 0:
                    P.emit("vector", lambda e: e.tensor_tensor(
                        CMX[:], CMX[:], BCA[:], OP.max),
                        waits=[(sV, P.c(sV))], inc=(sV, 1))
                wq_state["buf_free"][b] = P.c(sV)
            return (dma, vec)

        def wq_tail():
            def go(first_wait):
                P.emit("vector", lambda e: e.tensor_copy(HLV[0:64, :], CMX[64:128, :]),
                       waits=[(sV, P.c(sV))], inc=(sV, 1))
                P.emit("vector", lambda e: e.tensor_tensor(
                    CMX[0:64, :], CMX[0:64, :], HLV[0:64, :], OP.max),
                    waits=[(sV, P.c(sV))], inc=(sV, 1))
                P.emit("vector", lambda e: e.tensor_copy(HLV[0:32, :], CMX[32:64, :]),
                       inc=(sV, 1))
                P.emit("vector", lambda e: e.tensor_tensor(
                    CMX[0:32, :], CMX[0:32, :], HLV[0:32, :], OP.max),
                    waits=[(sV, P.c(sV))], inc=(sV, 1))
                for p in (16, 8, 4, 2, 1):
                    v_now = P.c(sV)
                    P.emit("sync", lambda e, p=p: e.dma_start(
                        out=HLV[0:p, :], in_=CMX[p:2 * p, :]),
                        waits=[(sV, v_now)], inc=(sL, 16))
                    wld = P.c(sL)
                    P.emit("vector", lambda e, p=p: e.tensor_tensor(
                        CMX[0:p, :], CMX[0:p, :], HLV[0:p, :], OP.max),
                        waits=[(sL, wld)], inc=(sV, 1))
                vv(lambda e: e.tensor_scalar(CMX[0:1, :], CMX[0:1, :], TINY,
                                             None, OP.max))
                vv(lambda e: e.tensor_scalar(BCA[0:1, :], CMX[0:1, :],
                                             1.0 / 127.0, None, OP.mult))
                vv(lambda e: e.reciprocal(CMX[0:1, :], CMX[0:1, :]))
                vv(lambda e: e.tensor_scalar(CMX[0:1, :], CMX[0:1, :], 127.0,
                                             None, OP.mult))
            return (None, go)

        def wq_bcast(c):
            def go(first_wait):
                v_r = P.c(sV)
                P.emit("tensor", lambda e, c=c: e.matmul(
                    bank(7)[0:64, :], ONE[0:1, 0:64],
                    CMX[0:1, c * 512:(c + 1) * 512], start=True, stop=True),
                    waits=[(sV, v_r)], inc=(sT, 1))
                P.emit("tensor", lambda e, c=c: e.matmul(
                    bank(7)[64:128, :], ONE[0:1, 0:64],
                    BCA[0:1, c * 512:(c + 1) * 512], start=True, stop=True),
                    inc=(sT, 1))
                t_r = P.c(sT)
                P.emit("vector", lambda e, c=c: e.tensor_copy(
                    HLV[0:64, c * 512:(c + 1) * 512], bank(7)[0:64, :]),
                    waits=[(sT, t_r)], inc=(sV, 1))
                P.emit("vector", lambda e, c=c: e.tensor_copy(
                    HLV[64:128, c * 512:(c + 1) * 512], bank(7)[64:128, :]),
                    inc=(sV, 1))
            return (None, go)

        def wq_bc_expand():
            def go(first_wait):
                v_r = P.c(sV)
                P.emit("vector", lambda e: e.tensor_copy(CMX[0:64, :], HLV[0:64, :]),
                       waits=[(sV, v_r)], inc=(sV, 1))
                P.emit("vector", lambda e: e.tensor_copy(CMX[64:128, :], HLV[0:64, :]),
                       inc=(sV, 1))
                P.emit("vector", lambda e: e.tensor_copy(BCA[0:64, :], HLV[64:128, :]),
                       inc=(sV, 1))
                P.emit("vector", lambda e: e.tensor_copy(BCA[64:128, :], HLV[64:128, :]),
                       inc=(sV, 1))
            return (None, go)

        def wq_quant(i):
            def dma(first_wait):
                b = i % 2
                P.emit("sync", lambda e, i=i, b=b: e.dma_start(
                    out=KF[b][:], in_=fk_d[i * 128:(i + 1) * 128, :]),
                    waits=[(sV, wq_state["buf_free"][b])], inc=(sL, 16))
                wq_state["ldq" + str(i)] = P.c(sL)

            def vec(first_wait):
                b = i % 2
                wld = wq_state["ldq" + str(i)]
                P.emit("vector", lambda e, b=b: e.tensor_tensor(
                    KF[b][:], KF[b][:], CMX[:], OP.mult),
                    waits=[(sL, wld)], inc=(sV, 1))
                for hh in range(2):
                    P.emit("vector", lambda e, b=b, hh=hh: e.tensor_copy(
                        AR4i[:], KF[b][:, hh * 1024:(hh + 1) * 1024]),
                        inc=(sV, 1))
                    P.emit("vector", lambda e, b=b, hh=hh: e.tensor_tensor(
                        KF[b][:, hh * 1024:(hh + 1) * 1024], AR4i[:],
                        BCA[:, hh * 1024:(hh + 1) * 1024], OP.mult), inc=(sV, 1))
                P.emit("vector", lambda e, b=b: e.tensor_copy(KQST[:], KF[b][:]),
                       waits=[(sS, wq_state["ar4_free"])], inc=(sV, 1))
                vq = P.c(sV)
                wq_state["buf_free"][b] = vq
                P.emit("sync", lambda e, i=i: e.dma_start(
                    out=kq_s[i * 128:(i + 1) * 128, :], in_=KQST[:]),
                    waits=[(sV, vq)], inc=(sS, 16))
                wq_state["ar4_free"] = P.c(sS)
            return (dma, vec)

        for _i in range(NT):
            wq_chunks.append(wq_amax(_i))
        wq_chunks.append(wq_tail())
        for _c in range(4):
            wq_chunks.append(wq_bcast(_c))
        wq_chunks.append(wq_bc_expand())
        for _i in range(NT):
            wq_chunks.append(wq_quant(_i))
        st_kq = 0

        P.emit("gpsimd", lambda e: e.iota(STT[0:32, 12:13], [[0, 1]], base=0,
                                          channel_multiplier=1,
                                          allow_small_or_imprecise_dtypes=True),
               waits=[(sV, v_memset), (sS, st_bqrow)], inc=(sG, 1))
        P.emit("gpsimd", lambda e: e.iota(TRW[0:32, :], [[1, 2048]], base=0,
                                          channel_multiplier=0,
                                          allow_small_or_imprecise_dtypes=True),
               inc=(sG, 1))
        g_io = P.c(sG)
        P.emit("scalar", lambda e: e.activation(STT[0:32, 13:14], STT[0:32, 12:13],
                                                ACT.Exp, scale=-LN1E4_32),
               waits=[(sG, g_io)], inc=(sC, 1))
        c_if = P.c(sC)
        P.emit("vector", lambda e: e.tensor_scalar(TRW[0:32, :], TRW[0:32, :],
                                                   STT[0:32, 13:14], None, OP.mult),
               waits=[(sC, c_if), (sS, st_bqrow)], inc=(sV, 1))

        WY = AR3[0:32, 6144:8192]
        W2 = AR3[0:32, 2048:4096]
        WI = AR3i[0:32, 4096:6144]
        SINF = AR3[64:96, 6144:8192]
        COSF = AR3[96:128, 6144:8192]

        C1 = 6.28125
        C2 = float(2.0 * np.pi - 6.28125)

        def sin_table(dst, phase):
            ph = float(phase)
            wprev = P.c(sC)
            P.emit("vector", lambda e: e.tensor_scalar(WY[:], TRW[0:32, :], ph, INV2PI,
                                                       OP.add, OP.mult),
                   waits=[(sC, wprev)], inc=(sV, 1))
            P.emit("vector", lambda e: e.tensor_copy(WI[:], WY[:]), inc=(sV, 1))
            P.emit("vector", lambda e: e.tensor_scalar(W2[:], WI[:], C1, None,
                                                       OP.mult), inc=(sV, 1))
            P.emit("vector", lambda e: e.tensor_tensor(WY[:], TRW[0:32, :], W2[:],
                                                       OP.subtract), inc=(sV, 1))
            P.emit("vector", lambda e: e.tensor_scalar(W2[:], WI[:], C2, None,
                                                       OP.mult), inc=(sV, 1))
            P.emit("vector", lambda e: e.tensor_tensor(WY[:], WY[:], W2[:],
                                                       OP.subtract), inc=(sV, 1))
            if ph != 0.0:
                P.emit("vector", lambda e: e.tensor_scalar(WY[:], WY[:], ph, None,
                                                           OP.add), inc=(sV, 1))
            P.emit("vector", lambda e: e.tensor_scalar(W2[:], WY[:], PI_CL, -TWO_PI,
                                                       OP.is_gt, OP.mult),
                   inc=(sV, 1))
            P.emit("vector", lambda e: e.tensor_tensor(WY[:], WY[:], W2[:], OP.add),
                   inc=(sV, 1))
            P.emit("vector", lambda e: e.tensor_scalar(W2[:], WY[:], -PI_CL, TWO_PI,
                                                       OP.is_lt, OP.mult),
                   inc=(sV, 1))
            P.emit("vector", lambda e: e.tensor_tensor(WY[:], WY[:], W2[:], OP.add),
                   inc=(sV, 1))
            P.emit("vector", lambda e: e.tensor_scalar(WY[:], WY[:], -PI_CL, PI_CL,
                                                       OP.max, OP.min), inc=(sV, 1))
            va = P.c(sV)
            P.emit("scalar", lambda e, dst=dst: e.activation(dst, WY[:], ACT.Sin),
                   waits=[(sV, va)], inc=(sC, 1))

        sin_table(SINF[:], 0.0)
        sin_table(COSF[:], float(np.pi / 2.0))
        c_tab = P.c(sC)
        for r in range(4):
            r0 = r * 32
            P.emit("vector", lambda e, r0=r0: e.tensor_copy(COSP[r0:r0 + 32, :],
                                                            COSF[:]),
                   waits=[(sC, c_tab)] if r == 0 else (), inc=(sV, 1))
            if r % 2 == 0:
                P.emit("vector", lambda e, r0=r0: e.tensor_scalar(
                    SINP[r0:r0 + 32, :], SINF[:], -1.0, None, OP.mult), inc=(sV, 1))
            else:
                P.emit("vector", lambda e, r0=r0: e.tensor_copy(SINP[r0:r0 + 32, :],
                                                                SINF[:]),
                       inc=(sV, 1))
        v_tab = P.c(sV)

        for k in range(NF):
            P.emit("gpsimd", lambda e, k=k: e.dma_start(
                out=AR2[:, k * 512:(k + 1) * 512],
                in_=kva_d[k * 128:(k + 1) * 128, :]), inc=(sG, 16))
        for l in range(4):
            P.emit("gpsimd", lambda e, l=l: e.dma_start(
                out=AR2[:, 8192 + l * 512:8192 + (l + 1) * 512],
                in_=kvb_d[l * 128:(l + 1) * 128, :]), inc=(sG, 16))
        g_kvw = P.c(sG)
        KVAW = AR2[:, 0:8192]
        KVBW = AR2[:, 8192:10240]
        KVC = AR2[:, 10240:18432]

        AR1v = AR1.rearrange("p (f c) -> p f c", c=2048)
        evac_free = [0, 0]
        x_tile_done = []
        x_tr_t = []
        XPAIR = AR3[:, 0:4096].rearrange("p (s c) -> p s c", s=2)
        for i in range(NT):
            b = i % 2
            xst = AR3[:, b * 2048:(b + 1) * 2048]
            bst = AR4[:, b * 2048:(b + 1) * 2048]
            if b == 0:
                P.emit("sync", lambda e, i=i: e.dma_start(
                    out=XPAIR[:],
                    in_=x_d[i * 128:(i + 2) * 128, :].rearrange(
                        "(s p) c -> p s c", p=128)),
                    waits=[(sS, st_bqrow),
                           (sV, v_tab if i < 2 else x_tile_done[i - 1])],
                    inc=(sL, 16))
            wld = P.c(sL)
            P.emit("vector", lambda e, bst=bst, xst=xst: e.tensor_copy(bst[:], xst[:]),
                   waits=[(sL, wld), (sS, st_bqrow if i == 0 else 0),
                          (sT, x_tr_t[i - 2] if i >= 2 else 0)],
                   inc=(sV, 1))
            vc = P.c(sV)
            for half in range(2):
                pb = bankb(6 + half)
                w = [(sV, vc), (sV, evac_free[half]), (sL, ld_consts)]
                for f8 in range(8):
                    f = half * 8 + f8
                    P.emit("tensor", lambda e, pb=pb, f8=f8, f=f, bst=bst: e.transpose(
                        pb[:, f8 * 128:(f8 + 1) * 128],
                        bst[:, f * 128:(f + 1) * 128], IDT[:]),
                        waits=w if f8 == 0 else (),
                        inc=(sT, 1) if f8 == 7 else None)
                ttr = P.c(sT)
                P.emit("vector", lambda e, pb=pb, i=i, half=half: e.tensor_copy(
                    AR1v[:, half * 8:half * 8 + 8, i * 128:(i + 1) * 128],
                    pb.rearrange("p (f c) -> p f c", c=128)),
                    waits=[(sT, ttr)], inc=(sV, 1))
                evac_free[half] = P.c(sV)
            x_tr_t.append(P.c(sT))
            x_tile_done.append(P.c(sV))
        v_xt = P.c(sV)

        kvc_bank_free = [0, 0]
        for m in range(4):
            for cc in range(4):
                bk = (m * 4 + cc) % 2
                wf = [(sV, v_xt), (sG, g_kvw), (sV, kvc_bank_free[bk])]
                for k in range(NF):
                    P.emit("tensor", lambda e, m=m, cc=cc, k=k, bk=bk: e.matmul(
                        bank(bk),
                        KVAW[:, k * 512 + m * 128:k * 512 + (m + 1) * 128],
                        AR1[:, k * 2048 + cc * 512:k * 2048 + (cc + 1) * 512],
                        start=(k == 0), stop=(k == NF - 1)),
                        waits=wf if k == 0 else (),
                        inc=(sT, 1) if k == NF - 1 else None)
                tmm = P.c(sT)
                P.emit("vector", lambda e, m=m, cc=cc, bk=bk: e.tensor_copy(
                    KVC[:, m * 2048 + cc * 512:m * 2048 + (cc + 1) * 512], bank(bk)),
                    waits=[(sT, tmm)], inc=(sV, 1))
                kvc_bank_free[bk] = P.c(sV)
        v_kvc = P.c(sV)

        KSW = AR3[0:64, 6144:8192]
        VTv = VT.rearrange("p (n c) -> p n c", c=65)
        vt_bank_free = [0, 0]
        for g in range(4):
            v_prev = P.c(sV)
            for cc in range(4):
                for l in range(4):
                    P.emit("tensor", lambda e, g=g, cc=cc, l=l: e.matmul(
                        bank(2 + cc),
                        KVBW[:, l * 512 + g * 128:l * 512 + (g + 1) * 128],
                        KVC[:, l * 2048 + cc * 512:l * 2048 + (cc + 1) * 512],
                        start=(l == 0), stop=(l == 3)),
                        waits=([(sV, v_kvc if g == 0 else v_prev)]
                               if (cc == 0 and l == 0) else ()),
                        inc=(sT, 1) if l == 3 else None)
            t_kv = P.c(sT)
            KVPS = PS[:, 1024:3072]
            P.emit("vector", lambda e: e.tensor_copy(KSW[0:32, :], KVPS[32:64, :]),
                   waits=[(sT, t_kv)], inc=(sV, 1))
            P.emit("vector", lambda e: e.tensor_copy(KSW[32:64, :], KVPS[0:32, :]),
                   inc=(sV, 1))
            P.emit("vector", lambda e, g=g: e.tensor_tensor(
                K2[0:64, g * 2048:(g + 1) * 2048], KVPS[0:64, :], COSP[0:64, :],
                OP.mult), inc=(sV, 1))
            P.emit("vector", lambda e: e.tensor_tensor(KSW[:], KSW[:], SINP[0:64, :],
                                                       OP.mult), inc=(sV, 1))
            P.emit("vector", lambda e, g=g: e.tensor_tensor(
                K2[0:64, g * 2048:(g + 1) * 2048],
                K2[0:64, g * 2048:(g + 1) * 2048], KSW[:], OP.add), inc=(sV, 1))
            P.emit("vector", lambda e, g=g: e.tensor_copy(
                K2[64:128, g * 2048:(g + 1) * 2048],
                K2[0:64, g * 2048:(g + 1) * 2048]), inc=(sV, 1))
            P.emit("vector", lambda e: e.tensor_copy(EB[0:64, 0:2048], KVPS[64:128, :]),
                   inc=(sV, 1))
            v_vst = P.c(sV)
            for half in range(2):
                pb = bankb(6 + half)
                w = [(sV, v_vst), (sV, vt_bank_free[half])]
                for k8 in range(8):
                    kt = half * 8 + k8
                    P.emit("tensor", lambda e, pb=pb, k8=k8, kt=kt: e.transpose(
                        pb[:, k8 * 64:(k8 + 1) * 64],
                        EB[0:64, kt * 128:(kt + 1) * 128], IDT[0:64, 0:64]),
                        waits=w if k8 == 0 else (),
                        inc=(sT, 1) if k8 == 7 else None)
                ttr = P.c(sT)
                P.emit("vector", lambda e, pb=pb, g=g, half=half: e.tensor_copy(
                    VTv[:, g * 16 + half * 8:g * 16 + half * 8 + 8, 0:64],
                    pb[:, 0:512].rearrange("p (n c) -> p n c", c=64)),
                    waits=[(sT, ttr)], inc=(sV, 1))
                vt_bank_free[half] = P.c(sV)
        v_kv_done = P.c(sV)

        QS = AR4[:, 0:2048]
        q_done_v = []
        for hp in range(NF):
            qc = hp * 2048
            w0 = [(sV, v_kv_done)] if hp == 0 else []
            P.emit("vector", lambda e, qc=qc: e.tensor_copy(
                QS[0:32, :], AR1[32:64, qc:qc + 2048]), waits=w0, inc=(sV, 1))
            P.emit("vector", lambda e, qc=qc: e.tensor_copy(
                QS[32:64, :], AR1[0:32, qc:qc + 2048]), inc=(sV, 1))
            P.emit("vector", lambda e, qc=qc: e.tensor_copy(
                QS[64:96, :], AR1[96:128, qc:qc + 2048]), inc=(sV, 1))
            P.emit("vector", lambda e, qc=qc: e.tensor_copy(
                QS[96:128, :], AR1[64:96, qc:qc + 2048]), inc=(sV, 1))
            P.emit("vector", lambda e, qc=qc: e.tensor_tensor(
                AR2[:, qc:qc + 2048], AR1[:, qc:qc + 2048], COSP[:], OP.mult),
                inc=(sV, 1))
            P.emit("vector", lambda e: e.tensor_tensor(QS[:], QS[:], SINP[:],
                                                       OP.mult), inc=(sV, 1))
            P.emit("vector", lambda e, qc=qc: e.tensor_tensor(
                AR2[:, qc:qc + 2048], AR2[:, qc:qc + 2048], QS[:], OP.add),
                inc=(sV, 1))
            q_done_v.append(P.c(sV))
        v_rope = P.c(sV)

        sc_bank_free = {0: 0, 2: 0}
        ov_bank_free = 0
        bc_bank_free = 0
        eslot_free = {0: 0, 1: 0, 2: 0}
        nr_free = 0
        un_free = 0
        pending_norm = None
        aot_last_v = 0
        ow_loaded_g = []
        for hp in range(NF):
            g = hp // 4
            qcol = hp * 2048
            for qb in range(4):
                kmax = 4 * qb + 4
                exp_cnt = {}
                diag_cnt = {}

                def emit_av(tp, last):
                    cp = tp - 4 * qb
                    qoffp = max(cp, 0) * 128
                    wV = [(sC, exp_cnt[tp])]
                    if tp == 0:
                        wV.append((sV, ov_bank_free))
                    P.emit("tensor", lambda e, g=g, tp=tp, qoffp=qoffp: e.matmul(
                        bank(4)[0:65, qoffp:512],
                        VT[:, (g * 16 + tp) * 65:(g * 16 + tp) * 65 + 65],
                        EB[:, (tp % 3) * 1024 + qoffp:(tp % 3) * 1024 + 512],
                        start=(tp == 0), stop=last),
                        waits=wV, inc=(sT, 1))
                    P.emit("tensor", lambda e, g=g, tp=tp, qoffp=qoffp: e.matmul(
                        bank(5)[0:65, qoffp:512],
                        VT[:, (g * 16 + tp) * 65:(g * 16 + tp) * 65 + 65],
                        EB[:, (tp % 3) * 1024 + 512 + qoffp:(tp % 3) * 1024 + 1024],
                        start=(tp == 0), stop=last),
                        inc=(sT, 1))
                    eslot_free[tp % 3] = P.c(sT)

                for t in range(kmax):
                    c = t - 4 * qb
                    qoff = max(c, 0) * 128
                    sb_ = (t % 2) * 2
                    wA = [(sC, sc_bank_free[sb_])]
                    if qb == 0 and t == 0:
                        wA.append((sV, q_done_v[hp]))
                    if c >= 0:
                        P.emit("tensor", lambda e, sb_=sb_, qoff=qoff: e.matmul(
                            bank(sb_)[:, qoff:qoff + 128], DGN[:], IDT[:],
                            start=True, stop=False), waits=wA, inc=(sT, 1))
                        P.emit("tensor", lambda e, sb_=sb_, qoff=qoff: e.matmul(
                            bank(sb_ + 1)[:, qoff:qoff + 128], DGN[:], IDT[:],
                            start=True, stop=False), inc=(sT, 1))
                        wA = []
                    P.emit("tensor", lambda e, g=g, t=t, sb_=sb_, qcol=qcol, qb=qb,
                           qoff=qoff, c=c: e.matmul(
                               bank(sb_)[:, qoff:512],
                               K2[0:64, g * 2048 + t * 128:g * 2048 + (t + 1) * 128],
                               AR2[0:64, qcol + qb * 512 + qoff:qcol + (qb + 1) * 512],
                               start=(c < 0), stop=True),
                           waits=wA, inc=(sT, 1))
                    P.emit("tensor", lambda e, g=g, t=t, sb_=sb_, qcol=qcol, qb=qb,
                           qoff=qoff, c=c: e.matmul(
                               bank(sb_ + 1)[:, qoff:512],
                               K2[64:128, g * 2048 + t * 128:g * 2048 + (t + 1) * 128],
                               AR2[64:128, qcol + qb * 512 + qoff:qcol + (qb + 1) * 512],
                               start=(c < 0), stop=True),
                           inc=(sT, 1))
                    mm_t = P.c(sT)
                    if t >= 2:
                        emit_av(t - 2, last=False)
                    wE = [(sT, max(mm_t, eslot_free[(t % 3)]))]
                    P.emit("scalar", lambda e, sb_=sb_, qoff=qoff, t=t: e.activation(
                        EB[:, (t % 3) * 1024:(t % 3 + 1) * 1024].rearrange(
                            "p (b n) -> p b n", n=512)[:, :, qoff:512],
                        PS[:, sb_ * 512:(sb_ + 2) * 512].rearrange(
                            "p (b n) -> p b n", n=512)[:, :, qoff:512],
                        ACT.Exp, bias=NEGB[:, 0:1], scale=QSCALE),
                        waits=wE, inc=(sC, 1))
                    exp_cnt[t] = P.c(sC)
                    sc_bank_free[sb_] = exp_cnt[t]
                    if t == 1 and pending_norm is not None:
                        pending_norm()
                        pending_norm = None
                    if t == 2:
                        it = hp * 4 + qb
                        if it == 0:
                            for ci in (0, 1):
                                if wq_chunks[ci][0]:
                                    wq_chunks[ci][0](v_rope)
                        if it < len(wq_chunks):
                            wq_chunks[it][1](v_rope)
                            if it + 2 < len(wq_chunks) and wq_chunks[it + 2][0]:
                                wq_chunks[it + 2][0](v_rope)
                emit_av(kmax - 2, last=False)
                emit_av(kmax - 1, last=True)
                t_av = P.c(sT)

                def make_norm(qcol=qcol, qb=qb, t_av=t_av):
                    def norm():
                        nonlocal nr_free, ov_bank_free, bc_bank_free, un_free
                        nonlocal aot_last_v
                        P.emit("vector", lambda e: e.tensor_copy(
                            NR[0:1, 0:1024], PS[64:65, 2048:3072]),
                            waits=[(sT, t_av), (sC, nr_free)], inc=(sV, 1))
                        v_ncp = P.c(sV)
                        P.emit("vector", lambda e: e.tensor_copy(
                            UNB[0:64, 0:512], bank(4)[0:64, :]),
                            waits=[(sV, un_free)], inc=(sV, 1))
                        P.emit("vector", lambda e: e.tensor_copy(
                            UNB[0:64, 512:1024], bank(5)[0:64, :]),
                            inc=(sV, 1))
                        ov_bank_free = P.c(sV)
                        P.emit("scalar", lambda e: e.activation(
                            NR[0:1, 0:1024], NR[0:1, 0:1024], ACT.Ln),
                            waits=[(sV, v_ncp)], inc=(sC, 1))
                        c_ln = P.c(sC)
                        nr_free = c_ln
                        P.emit("scalar", lambda e: e.activation(
                            NR[0:1, 0:1024], NR[0:1, 0:1024], ACT.Exp,
                            scale=-1.0), waits=[(sC, c_ln)], inc=(sC, 1))
                        c_nr = P.c(sC)
                        P.emit("tensor", lambda e: e.matmul(
                            bank(6)[0:64, :], ONE[0:1, 0:64], NR[0:1, 0:512],
                            start=True, stop=True),
                            waits=[(sC, c_nr), (sV, bc_bank_free)], inc=(sT, 1))
                        P.emit("tensor", lambda e: e.matmul(
                            bank(6)[64:128, :], ONE[0:1, 0:64], NR[0:1, 512:1024],
                            start=True, stop=True), inc=(sT, 1))
                        t_bc = P.c(sT)
                        P.emit("vector", lambda e: e.tensor_copy(
                            UNB[0:64, 1024:1536], bank(6)[0:64, :]),
                            waits=[(sT, t_bc)], inc=(sV, 1))
                        P.emit("vector", lambda e: e.tensor_copy(
                            UNB[0:64, 1536:2048], bank(6)[64:128, :]),
                            inc=(sV, 1))
                        bc_bank_free = P.c(sV)
                        P.emit("vector", lambda e: e.tensor_tensor(
                            AR1[0:64, qcol + qb * 512:qcol + (qb + 1) * 512],
                            UNB[0:64, 0:512], UNB[0:64, 1024:1536], OP.mult),
                            inc=(sV, 1))
                        P.emit("vector", lambda e: e.tensor_tensor(
                            AR1[64:128, qcol + qb * 512:qcol + (qb + 1) * 512],
                            UNB[0:64, 512:1024], UNB[0:64, 1536:2048], OP.mult),
                            inc=(sV, 1))
                        un_free = P.c(sV)
                        aot_last_v = P.c(sV)
                    return norm

                pending_norm = make_norm()
            last_score_mm = P.c(sT)
            P.emit("gpsimd", lambda e, hp=hp: e.dma_start(
                out=AR2[:, hp * 2048:(hp + 1) * 2048],
                in_=ow_d[hp * 128:(hp + 1) * 128, :]),
                waits=[(sT, last_score_mm)], inc=(sG, 16))
            ow_loaded_g.append(P.c(sG))
            if hp == NF - 1 and pending_norm is not None:
                pending_norm()
                pending_norm = None
        st_kq = P.c(sS)
        v_p2 = P.c(sV)

        t_p2_end = P.c(sT)
        P.emit("sync", lambda e: e.dma_start(out=K2F[0:1, 2048:4096], in_=bqr_s[:]),
               waits=[(sT, t_p2_end)], inc=(sL, 16))
        ld_bqrow = P.c(sL)
        bq_ev = 0
        for c in range(4):
            bk = c % 2
            P.emit("tensor", lambda e, c=c, bk=bk: e.matmul(
                bank(bk)[0:128, :], ONE[0:1, :],
                K2F[0:1, 2048 + c * 512:2048 + (c + 1) * 512],
                start=True, stop=True),
                waits=[(sL, ld_bqrow), (sT, t_p2_end), (sV, bq_ev)], inc=(sT, 1))
            t_one = P.c(sT)
            P.emit("vector", lambda e, c=c, bk=bk: e.tensor_copy(
                BQ[:, c * 512:(c + 1) * 512], bank(bk)),
                waits=[(sT, t_one), (sV, v_p2)], inc=(sV, 1))
            bq_ev = P.c(sV)
        v_bq_tile = P.c(sV)

        po_bank_free = [0, 0, 0, 0]
        tr_bank_free = [0, 0]
        x1_store = 0
        xqt_store = 0
        x1tr_t = 0
        pending_tr = None
        Y3 = AR3[:, 4096:6144]

        def emit_tr(jd, v_q1d):
            nonlocal_ = None
            for half in range(2):
                pb = bankb(6 + half)
                w = [(sV, v_q1d), (sV, tr_bank_free[half])]
                for f8 in range(8):
                    f = half * 8 + f8
                    P.emit("tensor", lambda e, pb=pb, f8=f8, f=f: e.transpose(
                        pb[:, f8 * 128:(f8 + 1) * 128],
                        AR4[:, f * 128:(f + 1) * 128], IDT[:]),
                        waits=w if f8 == 0 else (),
                        inc=(sT, 1) if f8 == 7 else None)
                ttr = P.c(sT)
                P.emit("vector", lambda e, pb=pb, half=half: e.tensor_copy(
                    AR4[:, 2048 + half * 1024:2048 + (half + 1) * 1024], pb[:]),
                    waits=[(sT, ttr), (sS, xqt_store)], inc=(sV, 1))
                tr_bank_free[half] = P.c(sV)
            v_tr = P.c(sV)
            P.emit("sync", lambda e, jd=jd: e.dma_start(
                out=xqt_s[jd * 128:(jd + 1) * 128, :], in_=AR4[:, 2048:4096]),
                waits=[(sV, v_tr)], inc=(sS, 16))
            return P.c(sT), P.c(sS)

        for j in range(NT):
            b = j % 2
            xres = AR3[:, b * 2048:(b + 1) * 2048]
            P.emit("gpsimd", lambda e, j=j, xres=xres: e.dma_start(
                out=xres[:], in_=x_d[j * 128:(j + 1) * 128, :]),
                waits=[(sV, v_p2 if j < 2 else po_bank_free[3])], inc=(sG, 16))
            w_xres = P.c(sG)
            for cc in range(4):
                wmm = [(sV, po_bank_free[cc]), (sV, aot_last_v),
                       (sG, ow_loaded_g[-1])]
                for k in range(NF):
                    P.emit("tensor", lambda e, j=j, cc=cc, k=k: e.matmul(
                        bank(2 + cc),
                        AR1[:, k * 2048 + j * 128:k * 2048 + (j + 1) * 128],
                        AR2[:, k * 2048 + cc * 512:k * 2048 + (cc + 1) * 512],
                        start=(k == 0), stop=(k == NF - 1)),
                        waits=wmm if k == 0 else (),
                        inc=(sT, 1) if k == NF - 1 else None)
            t_po = P.c(sT)
            if pending_tr is not None:
                x1tr_t, xqt_store = emit_tr(*pending_tr)
                pending_tr = None
            for cc in range(4):
                P.emit("vector", lambda e, cc=cc, xres=xres: e.tensor_tensor(
                    Y3[:, cc * 512:(cc + 1) * 512], bank(2 + cc),
                    xres[:, cc * 512:(cc + 1) * 512], OP.add),
                    waits=[(sT, t_po), (sS, x1_store), (sG, w_xres),
                           (sV, v_p2 if j < 2 else 0)],
                    inc=(sV, 1))
                po_bank_free[cc] = P.c(sV)
            v_y = P.c(sV)
            P.emit("scalar", lambda e: e.activation(AR4[:, 2048:4096], Y3[:],
                                                    ACT.Square,
                                                    accum_out=STT[:, 1:2]),
                   waits=[(sV, v_y), (sS, xqt_store)], inc=(sC, 1))
            c_sq = P.c(sC)
            P.emit("vector", lambda e: e.reduce_sum(STT[:, 0:1], Y3[:], axis=AX.X),
                   inc=(sV, 1))
            vv(lambda e: e.tensor_scalar(STT[:, 2:3], STT[:, 0:1],
                                         1.0 / D, None, OP.mult))
            vv(lambda e: e.tensor_tensor(STT[:, 4:5], STT[:, 2:3],
                                         STT[:, 2:3], OP.mult))
            vv(lambda e: e.scalar_tensor_tensor(STT[:, 3:4], STT[:, 1:2],
                                                1.0 / D, STT[:, 4:5],
                                                OP.mult, OP.subtract),
               extra=[(sC, c_sq)])
            v_var = P.c(sV)
            P.emit("scalar", lambda e: e.activation(STT[:, 9:10], STT[:, 3:4],
                                                    ACT.Ln, bias=STT[:, 14:15]),
                   waits=[(sV, v_var)], inc=(sC, 1))
            c_l1 = P.c(sC)
            P.emit("scalar", lambda e: e.activation(STT[:, 4:5], STT[:, 9:10],
                                                    ACT.Exp, scale=-0.5),
                   waits=[(sC, c_l1)], inc=(sC, 1))
            c_sd = P.c(sC)
            vv(lambda e: e.tensor_scalar(Y3[:], Y3[:], STT[:, 2:3],
                                         STT[:, 4:5], OP.subtract, OP.mult),
               extra=[(sC, c_sd)])
            v_x1 = P.c(sV)
            P.emit("sync", lambda e, j=j: e.dma_start(
                out=x1_s[j * 128:(j + 1) * 128, :], in_=Y3[:]),
                waits=[(sV, v_x1)], inc=(sS, 16))
            x1_store = P.c(sS)
            vv(lambda e: e.reduce_max(STT[:, 5:6], Y3[:], axis=AX.X,
                                      apply_absolute_value=True))
            vv(lambda e: e.tensor_scalar(STT[:, 5:6], STT[:, 5:6],
                                         TINY, None, OP.max))
            vv(lambda e: e.reciprocal(STT[:, 6:7], STT[:, 5:6]))
            vv(lambda e, j=j: e.tensor_scalar(
                S1B[:, j:j + 1], STT[:, 5:6], 1.0 / 127.0, None, OP.mult))
            vv(lambda e: e.tensor_scalar(AR3i[:, 6144:8192], Y3[:],
                                         STT[:, 6:7], 127.0, OP.mult, OP.mult),
               extra=[(sS, st_kq if j == 0 else 0)])
            P.emit("vector", lambda e: e.tensor_copy(AR4[:, 0:2048],
                                                     AR3i[:, 6144:8192]),
                   waits=[(sT, x1tr_t)], inc=(sV, 1))
            v_q1 = P.c(sV)
            pending_tr = (j, v_q1)
        x1tr_t, xqt_store = emit_tr(*pending_tr)
        st_p3 = P.c(sS)
        t_p3 = P.c(sT)

        for k in range(NF):
            P.emit("sync", lambda e, k=k: e.dma_start(
                out=AR2[:, k * 2048:(k + 1) * 2048],
                in_=kq_s[k * 128:(k + 1) * 128, :]),
                waits=[(sT, t_p3), (sS, st_p3)] if k == 0 else (), inc=(sL, 16))
        ld_kq = P.c(sL)
        pf_bank_free = [0, 0, 0, 0]
        pf_last_mm = [t_p3, t_p3]
        y2_done = [0, 0]
        yslot_free = [0, 0]
        ebi_free = 0
        for j in range(NT):
            b = j % 2
            xqst = AR4[:, b * 2048:(b + 1) * 2048]
            x1st = AR3[:, b * 2048:(b + 1) * 2048]
            P.emit("sync", lambda e, j=j, xqst=xqst: e.dma_start(
                out=xqst[:], in_=xqt_s[j * 128:(j + 1) * 128, :]),
                waits=([(sT, pf_last_mm[b])] if j >= 2 else
                       [(sT, t_p3), (sS, st_p3)]), inc=(sL, 16))
            P.emit("sync", lambda e, j=j, x1st=x1st: e.dma_start(
                out=x1st[:], in_=x1_s[j * 128:(j + 1) * 128, :]),
                waits=([(sV, y2_done[b])] if j >= 2 else [(sS, st_p3)]),
                inc=(sL, 16))
            w_ld = P.c(sL)
            for cc in range(4):
                wmm = [(sV, pf_bank_free[cc]), (sL, w_ld), (sL, ld_kq)]
                for k in range(NF):
                    P.emit("tensor", lambda e, cc=cc, k=k, xqst=xqst: e.matmul(
                        bank(2 + cc), xqst[:, k * 128:(k + 1) * 128],
                        AR2[:, k * 2048 + cc * 512:k * 2048 + (cc + 1) * 512],
                        start=(k == 0), stop=(k == NF - 1)),
                        waits=wmm if (k == 0 and cc == 0) else (),
                        inc=(sT, 1) if k == NF - 1 else None)
            pf_last_mm[b] = P.c(sT)
            t_pf = P.c(sT)
            Y = AR3[:, 4096 + b * 2048:4096 + (b + 1) * 2048]
            for cc in range(4):
                P.emit("vector", lambda e, cc=cc, j=j, Y=Y: e.scalar_tensor_tensor(
                    Y[:, cc * 512:(cc + 1) * 512], bank(2 + cc), S1B[:, j:j + 1],
                    BQ[:, cc * 512:(cc + 1) * 512], OP.mult, OP.add),
                    waits=[(sT, t_pf), (sS, yslot_free[b])], inc=(sV, 1))
                pf_bank_free[cc] = P.c(sV)
            P.emit("vector", lambda e, Y=Y: e.tensor_scalar(Y[:], Y[:], 0.0, None,
                                                            OP.max), inc=(sV, 1))
            vv(lambda e, Y=Y: e.reduce_max(
                STT[:, 5:6], Y[:], axis=AX.X, apply_absolute_value=True))
            vv(lambda e: e.tensor_scalar(STT[:, 5:6], STT[:, 5:6],
                                         TINY, None, OP.max))
            vv(lambda e: e.reciprocal(STT[:, 6:7], STT[:, 5:6]))
            vv(lambda e: e.tensor_scalar(STT[:, 7:8], STT[:, 5:6],
                                         1.0 / 127.0, None, OP.mult))
            for hh in range(2):
                vv(lambda e, hh=hh, Y=Y: e.tensor_scalar(
                    EBi[:, 0:1024], Y[:, hh * 1024:(hh + 1) * 1024], STT[:, 6:7],
                    127.0, OP.mult, OP.mult), extra=[(sC, ebi_free)])
                P.emit("vector", lambda e, hh=hh, Y=Y, x1st=x1st:
                       e.scalar_tensor_tensor(
                           Y[:, hh * 1024:(hh + 1) * 1024], EBi[:, 0:1024],
                           STT[:, 7:8], x1st[:, hh * 1024:(hh + 1) * 1024],
                           OP.mult, OP.add), inc=(sV, 1))
            v_y2 = P.c(sV)
            y2_done[b] = v_y2
            P.emit("scalar", lambda e, Y=Y: e.activation(EB[:, 0:2048], Y[:],
                                                         ACT.Square,
                                                         accum_out=STT[:, 1:2]),
                   waits=[(sV, v_y2)], inc=(sC, 1))
            c_sq = P.c(sC)
            ebi_free = c_sq
            P.emit("vector", lambda e, Y=Y: e.reduce_sum(STT[:, 0:1], Y[:],
                                                         axis=AX.X), inc=(sV, 1))
            vv(lambda e: e.tensor_scalar(STT[:, 2:3], STT[:, 0:1],
                                         1.0 / D, None, OP.mult))
            vv(lambda e: e.tensor_tensor(STT[:, 4:5], STT[:, 2:3],
                                         STT[:, 2:3], OP.mult))
            vv(lambda e: e.scalar_tensor_tensor(STT[:, 3:4], STT[:, 1:2],
                                                1.0 / D, STT[:, 4:5],
                                                OP.mult, OP.subtract),
               extra=[(sC, c_sq)])
            v_var = P.c(sV)
            P.emit("scalar", lambda e: e.activation(STT[:, 9:10], STT[:, 3:4],
                                                    ACT.Ln, bias=STT[:, 14:15]),
                   waits=[(sV, v_var)], inc=(sC, 1))
            c_l1 = P.c(sC)
            P.emit("scalar", lambda e: e.activation(STT[:, 4:5], STT[:, 9:10],
                                                    ACT.Exp, scale=-0.5),
                   waits=[(sC, c_l1)], inc=(sC, 1))
            c_sd = P.c(sC)
            vv(lambda e, Y=Y: e.tensor_scalar(
                Y[:], Y[:], STT[:, 2:3], STT[:, 4:5], OP.subtract, OP.mult),
               extra=[(sC, c_sd)])
            v_out = P.c(sV)
            P.emit("gpsimd", lambda e, j=j, Y=Y: e.dma_start(
                out=out_d[j * 128:(j + 1) * 128, :], in_=Y[:]),
                waits=[(sV, v_out)], inc=(sS, 16))
            yslot_free[b] = P.c(sS)
        st_total = P.c(sS)
        P.emit("sync", lambda e: e.nop(), waits=[(sS, st_total)])

        @blk.sync
        def _(e):
            for op in P.ops["sync"]:
                op(e)

        @blk.gpsimd
        def _(e):
            for op in P.ops["gpsimd"]:
                op(e)

        @blk.vector
        def _(e):
            for op in P.ops["vector"]:
                op(e)

        @blk.scalar
        def _(e):
            for op in P.ops["scalar"]:
                op(e)

        @blk.tensor
        def _(e):
            for op in P.ops["tensor"]:
                op(e)

    return nc


_NC_CACHE = None


def _consts():
    ident = np.eye(128, dtype=ml_dtypes.bfloat16)
    diag01 = np.triu(np.ones((128, 128), dtype=np.float32)).astype(ml_dtypes.bfloat16)
    kk = np.arange(128)[:, None]
    qq = np.arange(128)[None, :]
    mneg = np.where(kk > qq, np.float32(-1e30), np.float32(0.0))
    diagneg = np.ascontiguousarray(mneg.T).astype(ml_dtypes.bfloat16)
    return ident, diag01, diagneg


def kernel(L0_attn_hidden_states, L0_attn_kv_a_w, L0_attn_kv_b_w, L0_attn_o_w,
           L0_ffn_kernel, L0_ffn_bias):
    global _NC_CACHE
    hs = np.ascontiguousarray(np.asarray(L0_attn_hidden_states))
    kva = np.ascontiguousarray(np.asarray(L0_attn_kv_a_w))
    kvb = np.ascontiguousarray(np.asarray(L0_attn_kv_b_w))
    ow = np.ascontiguousarray(np.asarray(L0_attn_o_w))
    fk = np.ascontiguousarray(np.asarray(L0_ffn_kernel))
    fb = np.ascontiguousarray(np.asarray(L0_ffn_bias)).reshape(1, -1)
    ident, diag01, diagneg = _consts()
    if _NC_CACHE is None:
        _NC_CACHE = build_nc()
    in_maps = []
    for c in range(8):
        in_maps.append({
            "x": hs[c], "kv_a_w": kva, "kv_b_w": kvb, "o_w": ow,
            "fk": fk, "fb": fb, "ident": ident, "diag01": diag01,
            "diagneg": diagneg,
        })
    res = run_bass_kernel_spmd(_NC_CACHE, in_maps, core_ids=list(range(8)))
    return np.stack([np.asarray(res.results[c]["out"]) for c in range(8)], axis=0)


# version 22
# speedup vs baseline: 1.0433x; 1.0433x over previous
import numpy as np
import ml_dtypes
from contextlib import ExitStack

import concourse.bass as bass
import concourse.mybir as mybir
from concourse.alu_op_type import AluOpType as OP
from concourse.bass_utils import run_bass_kernel_spmd

F32 = mybir.dt.float32
BF16 = mybir.dt.bfloat16
I32 = mybir.dt.int32
ACT = mybir.ActivationFunctionType
AX = mybir.AxisListType

S = 2048
D = 2048
HD = 64
QSCALE = HD ** -0.5
EXPB = 10.0
NT = S // 128
NF = D // 128
TINY = 1e-30
INV2PI = float(1.0 / (2.0 * np.pi))
TWO_PI = float(2.0 * np.pi)
PI_CL = 3.1415925
LN1E4_32 = float(np.log(10000.0) / 32.0)


class Plan:
    def __init__(self):
        self.ops = {k: [] for k in ("sync", "tensor", "vector", "scalar", "gpsimd")}
        self.cnt = {}

    def emit(self, eng, fn, waits=(), inc=None):
        waits = tuple((s, v) for (s, v) in waits if v > 0)

        def op(e, _fn=fn, _w=waits, _i=inc):
            for sem, val in _w:
                e.wait_ge(sem, val)
            r = _fn(e)
            if _i is not None:
                r.then_inc(_i[0], _i[1])
        self.ops[eng].append(op)
        if inc is not None:
            self.cnt[id(inc[0])] = self.cnt.get(id(inc[0]), 0) + inc[1]

    def c(self, sem):
        return self.cnt.get(id(sem), 0)


def build_nc():
    nc = bass.Bass("TRN2", target_bir_lowering=False)
    x_d = nc.dram_tensor("x", [S, D], F32, kind="ExternalInput").ap()
    kva_d = nc.dram_tensor("kv_a_w", [D, 512], F32, kind="ExternalInput").ap()
    kvb_d = nc.dram_tensor("kv_b_w", [512, 512], F32, kind="ExternalInput").ap()
    ow_d = nc.dram_tensor("o_w", [D, D], F32, kind="ExternalInput").ap()
    fk_d = nc.dram_tensor("fk", [D, D], F32, kind="ExternalInput").ap()
    fb_d = nc.dram_tensor("fb", [1, D], F32, kind="ExternalInput").ap()
    id_d = nc.dram_tensor("ident", [128, 128], BF16, kind="ExternalInput").ap()
    dg_d = nc.dram_tensor("diag01", [128, 128], BF16, kind="ExternalInput").ap()
    dn_d = nc.dram_tensor("diagneg", [128, 128], BF16, kind="ExternalInput").ap()
    cosp_d = nc.dram_tensor("cosp", [128, 2048], BF16, kind="ExternalInput").ap()
    sinp_d = nc.dram_tensor("sinp", [128, 2048], BF16, kind="ExternalInput").ap()
    out_d = nc.dram_tensor("out", [S, D], F32, kind="ExternalOutput").ap()
    kq_s = nc.dram_tensor("kq_scr", [D, D], BF16, kind="Internal").ap()
    x1_s = nc.dram_tensor("x1_scr", [S, D], F32, kind="Internal").ap()
    xqt_s = nc.dram_tensor("xqt_scr", [S, D], BF16, kind="Internal").ap()
    bqr_s = nc.dram_tensor("bqr_scr", [1, D], F32, kind="Internal").ap()

    es = ExitStack()
    with es:
        def sbt(name, shape, dt):
            return es.enter_context(nc.sbuf_tensor(name, shape, dt))

        AR1 = sbt("AR1", [128, 32768], BF16)
        AR2 = sbt("AR2", [128, 32768], BF16)
        AR3 = sbt("AR3", [128, 8192], F32)
        AR4 = sbt("AR4", [128, 4096], BF16)
        K2 = sbt("K2", [128, 4 * S], BF16)
        VT = sbt("VT", [128, 4 * NT * 65], BF16)
        EB = sbt("EB", [128, 3072], BF16)
        IDT = sbt("IDT", [128, 128], BF16)
        DG = sbt("DG", [128, 128], BF16)
        DGN = sbt("DGN", [128, 128], BF16)
        ONE = sbt("ONE", [64, 128], F32)
        NR = sbt("NR", [64, 1024], F32)
        UNB = sbt("UNB", [64, 2048], BF16)
        STT = sbt("STT", [128, 16], F32)
        S1B = sbt("S1B", [128, NT], F32)
        NEGB = sbt("NEGB", [128, 1], F32)
        PS = es.enter_context(nc.psum_tensor("PS", [128, 4096], F32))

        sL = es.enter_context(nc.semaphore("sL"))
        sG = es.enter_context(nc.semaphore("sG"))
        sS = es.enter_context(nc.semaphore("sSt"))
        sT = es.enter_context(nc.semaphore("sT"))
        sV = es.enter_context(nc.semaphore("sV"))
        sC = es.enter_context(nc.semaphore("sC"))
        blk = es.enter_context(nc.Block())

        P = Plan()
        AR3i = AR3.bitcast(I32)
        AR3b = AR3.bitcast(BF16)
        EBi = EB.bitcast(I32)
        COSP = AR3b[:, 8192:10240]
        SINP = AR3b[:, 10240:12288]
        HLV = AR4.bitcast(F32)[0:128, 0:2048]
        TRW = AR3[0:32, 0:2048]
        BQ = K2[:, 0:2048]
        K2F = K2.bitcast(F32)

        def bank(i):
            return PS[:, i * 512:(i + 1) * 512]

        def bankb(i):
            return PS.bitcast(BF16)[:, i * 1024:(i + 1) * 1024]

        def vfence():
            P.emit("vector", lambda e: e.tensor_copy(STT[0:1, 15:16], STT[0:1, 15:16]),
                   inc=(sV, 1))
            return P.c(sV)

        def vv(fn2, extra=()):
            P.emit("vector", fn2, waits=[(sV, P.c(sV))] + list(extra), inc=(sV, 1))

        P.emit("sync", lambda e: e.dma_start(out=IDT[:], in_=id_d[:]), inc=(sL, 16))
        P.emit("sync", lambda e: e.dma_start(out=DG[:], in_=dg_d[:]), inc=(sL, 16))
        P.emit("sync", lambda e: e.dma_start(out=DGN[:], in_=dn_d[:]), inc=(sL, 16))
        P.emit("sync", lambda e: e.dma_start(out=AR3[0:1, 0:D], in_=fb_d[:]),
               inc=(sL, 16))
        ld_consts = P.c(sL)
        P.emit("vector", lambda e: e.memset(NEGB[:], -EXPB), inc=(sV, 1))
        P.emit("vector", lambda e: e.memset(VT[:], 1.0), inc=(sV, 1))
        P.emit("vector", lambda e: e.memset(NR[:], 0.0), inc=(sV, 1))
        P.emit("vector", lambda e: e.memset(STT[:], 0.0), inc=(sV, 1))
        P.emit("vector", lambda e: e.memset(STT[:, 14:15], 1e-5), inc=(sV, 1))
        P.emit("vector", lambda e: e.memset(ONE[:], 0.0), inc=(sV, 1))
        P.emit("vector", lambda e: e.memset(ONE[0:1, :], 1.0), inc=(sV, 1))
        P.emit("vector", lambda e: e.memset(ONE[32:33, :], 1.0), inc=(sV, 1))
        v_memset = P.c(sV)

        fbrow = AR3[0:1, 0:D]
        P.emit("vector", lambda e: e.reduce_max(STT[0:1, 8:9], fbrow, axis=AX.X,
                                                apply_absolute_value=True),
               waits=[(sL, ld_consts)], inc=(sV, 1))
        vv(lambda e: e.tensor_scalar(STT[0:1, 8:9], STT[0:1, 8:9],
                                     TINY, None, OP.max))
        vv(lambda e: e.reciprocal(STT[0:1, 9:10], STT[0:1, 8:9]))
        vv(lambda e: e.tensor_scalar(STT[0:1, 9:10], STT[0:1, 9:10],
                                     127.0, None, OP.mult))
        vv(lambda e: e.tensor_scalar(STT[0:1, 10:11], STT[0:1, 8:9],
                                     1.0 / 127.0, None, OP.mult))
        vv(lambda e: e.tensor_scalar(AR3i[0:1, 2048:2048 + D], fbrow,
                                     STT[0:1, 9:10], None, OP.mult))
        vv(lambda e: e.tensor_scalar(AR3[0:1, 2048:2048 + D],
                                     AR3i[0:1, 2048:2048 + D],
                                     STT[0:1, 10:11], None, OP.mult))
        v_bqrow = P.c(sV)
        P.emit("sync", lambda e: e.dma_start(out=bqr_s[:], in_=AR3[0:1, 2048:2048 + D]),
               waits=[(sV, v_bqrow)], inc=(sS, 16))
        st_bqrow = P.c(sS)
        v_consts_done = P.c(sV)

        KF = [AR3[:, 0:2048], AR3[:, 2048:4096]]
        CMX = AR3[:, 4096:6144]
        BCA = AR3[:, 6144:8192]
        AR4i = AR4.bitcast(I32)[:, 0:1024]
        KQST = AR4[:, 2048:4096]
        wq_chunks = []
        wq_state = {"buf_free": [0, 0], "ar4_free": 0, "st_kq": 0}

        def wq_amax(i):
            def dma(first_wait):
                b = i % 2
                P.emit("sync", lambda e, i=i, b=b: e.dma_start(
                    out=KF[b][:], in_=fk_d[i * 128:(i + 1) * 128, :]),
                    waits=[(sV, max(wq_state["buf_free"][b], first_wait))],
                    inc=(sL, 16))
                wq_state["ld" + str(i)] = P.c(sL)

            def vec(first_wait):
                b = i % 2
                wld = wq_state["ld" + str(i)]
                dst = CMX if i == 0 else BCA
                P.emit("vector", lambda e, b=b, dst=dst: e.scalar_tensor_tensor(
                    dst[:, 0:2048], KF[b][:, 0:2048], -1.0, KF[b][:, 0:2048],
                    OP.mult, OP.max),
                    waits=[(sL, wld)], inc=(sV, 1))
                if i > 0:
                    P.emit("vector", lambda e: e.tensor_tensor(
                        CMX[:], CMX[:], BCA[:], OP.max),
                        waits=[(sV, P.c(sV))], inc=(sV, 1))
                wq_state["buf_free"][b] = P.c(sV)
            return (dma, vec)

        def wq_tail():
            def go(first_wait):
                P.emit("vector", lambda e: e.tensor_copy(HLV[0:64, :], CMX[64:128, :]),
                       waits=[(sV, P.c(sV))], inc=(sV, 1))
                P.emit("vector", lambda e: e.tensor_tensor(
                    CMX[0:64, :], CMX[0:64, :], HLV[0:64, :], OP.max),
                    waits=[(sV, P.c(sV))], inc=(sV, 1))
                P.emit("vector", lambda e: e.tensor_copy(HLV[0:32, :], CMX[32:64, :]),
                       inc=(sV, 1))
                P.emit("vector", lambda e: e.tensor_tensor(
                    CMX[0:32, :], CMX[0:32, :], HLV[0:32, :], OP.max),
                    waits=[(sV, P.c(sV))], inc=(sV, 1))
                for p in (16, 8, 4, 2, 1):
                    v_now = P.c(sV)
                    P.emit("sync", lambda e, p=p: e.dma_start(
                        out=HLV[0:p, :], in_=CMX[p:2 * p, :]),
                        waits=[(sV, v_now)], inc=(sL, 16))
                    wld = P.c(sL)
                    P.emit("vector", lambda e, p=p: e.tensor_tensor(
                        CMX[0:p, :], CMX[0:p, :], HLV[0:p, :], OP.max),
                        waits=[(sL, wld)], inc=(sV, 1))
                vv(lambda e: e.tensor_scalar(CMX[0:1, :], CMX[0:1, :], TINY,
                                             None, OP.max))
                vv(lambda e: e.tensor_scalar(BCA[0:1, :], CMX[0:1, :],
                                             1.0 / 127.0, None, OP.mult))
                vv(lambda e: e.reciprocal(CMX[0:1, :], CMX[0:1, :]))
                vv(lambda e: e.tensor_scalar(CMX[0:1, :], CMX[0:1, :], 127.0,
                                             None, OP.mult))
            return (None, go)

        def wq_bcast(c):
            def go(first_wait):
                v_r = P.c(sV)
                P.emit("tensor", lambda e, c=c: e.matmul(
                    bank(7)[0:64, :], ONE[0:1, 0:64],
                    CMX[0:1, c * 512:(c + 1) * 512], start=True, stop=True),
                    waits=[(sV, v_r)], inc=(sT, 1))
                P.emit("tensor", lambda e, c=c: e.matmul(
                    bank(7)[64:128, :], ONE[0:1, 0:64],
                    BCA[0:1, c * 512:(c + 1) * 512], start=True, stop=True),
                    inc=(sT, 1))
                t_r = P.c(sT)
                P.emit("vector", lambda e, c=c: e.tensor_copy(
                    HLV[0:64, c * 512:(c + 1) * 512], bank(7)[0:64, :]),
                    waits=[(sT, t_r)], inc=(sV, 1))
                P.emit("vector", lambda e, c=c: e.tensor_copy(
                    HLV[64:128, c * 512:(c + 1) * 512], bank(7)[64:128, :]),
                    inc=(sV, 1))
            return (None, go)

        def wq_bc_expand():
            def go(first_wait):
                v_r = P.c(sV)
                P.emit("vector", lambda e: e.tensor_copy(CMX[0:64, :], HLV[0:64, :]),
                       waits=[(sV, v_r)], inc=(sV, 1))
                P.emit("vector", lambda e: e.tensor_copy(CMX[64:128, :], HLV[0:64, :]),
                       inc=(sV, 1))
                P.emit("vector", lambda e: e.tensor_copy(BCA[0:64, :], HLV[64:128, :]),
                       inc=(sV, 1))
                P.emit("vector", lambda e: e.tensor_copy(BCA[64:128, :], HLV[64:128, :]),
                       inc=(sV, 1))
            return (None, go)

        def wq_quant(i):
            def dma(first_wait):
                b = i % 2
                P.emit("sync", lambda e, i=i, b=b: e.dma_start(
                    out=KF[b][:], in_=fk_d[i * 128:(i + 1) * 128, :]),
                    waits=[(sV, wq_state["buf_free"][b])], inc=(sL, 16))
                wq_state["ldq" + str(i)] = P.c(sL)

            def vec(first_wait):
                b = i % 2
                wld = wq_state["ldq" + str(i)]
                P.emit("vector", lambda e, b=b: e.tensor_tensor(
                    KF[b][:], KF[b][:], CMX[:], OP.mult),
                    waits=[(sL, wld)], inc=(sV, 1))
                for hh in range(2):
                    P.emit("vector", lambda e, b=b, hh=hh: e.tensor_copy(
                        AR4i[:], KF[b][:, hh * 1024:(hh + 1) * 1024]),
                        inc=(sV, 1))
                    P.emit("vector", lambda e, b=b, hh=hh: e.tensor_tensor(
                        KF[b][:, hh * 1024:(hh + 1) * 1024], AR4i[:],
                        BCA[:, hh * 1024:(hh + 1) * 1024], OP.mult), inc=(sV, 1))
                P.emit("vector", lambda e, b=b: e.tensor_copy(KQST[:], KF[b][:]),
                       waits=[(sS, wq_state["ar4_free"])], inc=(sV, 1))
                vq = P.c(sV)
                wq_state["buf_free"][b] = vq
                P.emit("sync", lambda e, i=i: e.dma_start(
                    out=kq_s[i * 128:(i + 1) * 128, :], in_=KQST[:]),
                    waits=[(sV, vq)], inc=(sS, 16))
                wq_state["ar4_free"] = P.c(sS)
            return (dma, vec)

        for _i in range(NT):
            wq_chunks.append(wq_amax(_i))
        wq_chunks.append(wq_tail())
        for _c in range(4):
            wq_chunks.append(wq_bcast(_c))
        wq_chunks.append(wq_bc_expand())
        for _i in range(NT):
            wq_chunks.append(wq_quant(_i))
        st_kq = 0

        P.emit("sync", lambda e: e.dma_start(out=COSP[:], in_=cosp_d[:]),
               inc=(sL, 16))
        P.emit("sync", lambda e: e.dma_start(out=SINP[:], in_=sinp_d[:]),
               inc=(sL, 16))
        ld_tab = P.c(sL)
        P.emit("vector", lambda e: e.tensor_copy(STT[0:1, 15:16], STT[0:1, 15:16]),
               waits=[(sL, ld_tab)], inc=(sV, 1))
        v_tab = P.c(sV)

        for k in range(NF):
            P.emit("gpsimd", lambda e, k=k: e.dma_start(
                out=AR2[:, k * 512:(k + 1) * 512],
                in_=kva_d[k * 128:(k + 1) * 128, :]), inc=(sG, 16))
        for l in range(4):
            P.emit("gpsimd", lambda e, l=l: e.dma_start(
                out=AR2[:, 8192 + l * 512:8192 + (l + 1) * 512],
                in_=kvb_d[l * 128:(l + 1) * 128, :]), inc=(sG, 16))
        g_kvw = P.c(sG)
        KVAW = AR2[:, 0:8192]
        KVBW = AR2[:, 8192:10240]
        KVC = AR2[:, 10240:18432]

        AR1v = AR1.rearrange("p (f c) -> p f c", c=2048)
        evac_free = [0, 0]
        x_tile_done = []
        x_tr_t = []
        for i in range(NT):
            b = i % 2
            xst = AR3[:, b * 2048:(b + 1) * 2048]
            bst = AR4[:, b * 2048:(b + 1) * 2048]
            P.emit("sync", lambda e, i=i, xst=xst: e.dma_start(
                out=xst[:], in_=x_d[i * 128:(i + 1) * 128, :]),
                waits=[(sS, st_bqrow),
                       (sV, v_tab if i < 2 else x_tile_done[i - 2])],
                inc=(sL, 16))
            wld = P.c(sL)
            P.emit("vector", lambda e, bst=bst, xst=xst: e.tensor_copy(bst[:], xst[:]),
                   waits=[(sL, wld), (sS, st_bqrow if i == 0 else 0),
                          (sT, x_tr_t[i - 2] if i >= 2 else 0)],
                   inc=(sV, 1))
            vc = P.c(sV)
            for half in range(2):
                pb = bankb(6 + half)
                w = [(sV, vc), (sV, evac_free[half]), (sL, ld_consts)]
                for f8 in range(8):
                    f = half * 8 + f8
                    P.emit("tensor", lambda e, pb=pb, f8=f8, f=f, bst=bst: e.transpose(
                        pb[:, f8 * 128:(f8 + 1) * 128],
                        bst[:, f * 128:(f + 1) * 128], IDT[:]),
                        waits=w if f8 == 0 else (),
                        inc=(sT, 1) if f8 == 7 else None)
                ttr = P.c(sT)
                P.emit("vector", lambda e, pb=pb, i=i, half=half: e.tensor_copy(
                    AR1v[:, half * 8:half * 8 + 8, i * 128:(i + 1) * 128],
                    pb.rearrange("p (f c) -> p f c", c=128)),
                    waits=[(sT, ttr)], inc=(sV, 1))
                evac_free[half] = P.c(sV)
            x_tr_t.append(P.c(sT))
            x_tile_done.append(P.c(sV))
        v_xt = P.c(sV)

        kvc_bank_free = [0, 0]
        for m in range(4):
            for cc in range(4):
                bk = (m * 4 + cc) % 2
                wf = [(sV, v_xt), (sG, g_kvw), (sV, kvc_bank_free[bk])]
                for k in range(NF):
                    P.emit("tensor", lambda e, m=m, cc=cc, k=k, bk=bk: e.matmul(
                        bank(bk),
                        KVAW[:, k * 512 + m * 128:k * 512 + (m + 1) * 128],
                        AR1[:, k * 2048 + cc * 512:k * 2048 + (cc + 1) * 512],
                        start=(k == 0), stop=(k == NF - 1)),
                        waits=wf if k == 0 else (),
                        inc=(sT, 1) if k == NF - 1 else None)
                tmm = P.c(sT)
                P.emit("vector", lambda e, m=m, cc=cc, bk=bk: e.tensor_copy(
                    KVC[:, m * 2048 + cc * 512:m * 2048 + (cc + 1) * 512], bank(bk)),
                    waits=[(sT, tmm)], inc=(sV, 1))
                kvc_bank_free[bk] = P.c(sV)
        v_kvc = P.c(sV)

        KSW = AR3[0:64, 6144:8192]
        VTv = VT.rearrange("p (n c) -> p n c", c=65)
        vt_bank_free = [0, 0]
        for g in range(4):
            v_prev = P.c(sV)
            for cc in range(4):
                for l in range(4):
                    P.emit("tensor", lambda e, g=g, cc=cc, l=l: e.matmul(
                        bank(2 + cc),
                        KVBW[:, l * 512 + g * 128:l * 512 + (g + 1) * 128],
                        KVC[:, l * 2048 + cc * 512:l * 2048 + (cc + 1) * 512],
                        start=(l == 0), stop=(l == 3)),
                        waits=([(sV, v_kvc if g == 0 else v_prev)]
                               if (cc == 0 and l == 0) else ()),
                        inc=(sT, 1) if l == 3 else None)
            t_kv = P.c(sT)
            KVPS = PS[:, 1024:3072]
            P.emit("vector", lambda e: e.tensor_copy(KSW[0:32, :], KVPS[32:64, :]),
                   waits=[(sT, t_kv)], inc=(sV, 1))
            P.emit("vector", lambda e: e.tensor_copy(KSW[32:64, :], KVPS[0:32, :]),
                   inc=(sV, 1))
            P.emit("vector", lambda e, g=g: e.tensor_tensor(
                K2[0:64, g * 2048:(g + 1) * 2048], KVPS[0:64, :], COSP[0:64, :],
                OP.mult), inc=(sV, 1))
            P.emit("vector", lambda e: e.tensor_tensor(KSW[:], KSW[:], SINP[0:64, :],
                                                       OP.mult), inc=(sV, 1))
            P.emit("vector", lambda e, g=g: e.tensor_tensor(
                K2[0:64, g * 2048:(g + 1) * 2048],
                K2[0:64, g * 2048:(g + 1) * 2048], KSW[:], OP.add), inc=(sV, 1))
            P.emit("vector", lambda e, g=g: e.tensor_copy(
                K2[64:128, g * 2048:(g + 1) * 2048],
                K2[0:64, g * 2048:(g + 1) * 2048]), inc=(sV, 1))
            P.emit("vector", lambda e: e.tensor_copy(EB[0:64, 0:2048], KVPS[64:128, :]),
                   inc=(sV, 1))
            v_vst = P.c(sV)
            for half in range(2):
                pb = bankb(6 + half)
                w = [(sV, v_vst), (sV, vt_bank_free[half])]
                for k8 in range(8):
                    kt = half * 8 + k8
                    P.emit("tensor", lambda e, pb=pb, k8=k8, kt=kt: e.transpose(
                        pb[:, k8 * 64:(k8 + 1) * 64],
                        EB[0:64, kt * 128:(kt + 1) * 128], IDT[0:64, 0:64]),
                        waits=w if k8 == 0 else (),
                        inc=(sT, 1) if k8 == 7 else None)
                ttr = P.c(sT)
                P.emit("vector", lambda e, pb=pb, g=g, half=half: e.tensor_copy(
                    VTv[:, g * 16 + half * 8:g * 16 + half * 8 + 8, 0:64],
                    pb[:, 0:512].rearrange("p (n c) -> p n c", c=64)),
                    waits=[(sT, ttr)], inc=(sV, 1))
                vt_bank_free[half] = P.c(sV)
        v_kv_done = P.c(sV)

        QS = AR4[:, 0:2048]
        q_done_v = []
        for hp in range(NF):
            qc = hp * 2048
            w0 = [(sV, v_kv_done)] if hp == 0 else []
            P.emit("vector", lambda e, qc=qc: e.tensor_copy(
                QS[0:32, :], AR1[32:64, qc:qc + 2048]), waits=w0, inc=(sV, 1))
            P.emit("vector", lambda e, qc=qc: e.tensor_copy(
                QS[32:64, :], AR1[0:32, qc:qc + 2048]), inc=(sV, 1))
            P.emit("vector", lambda e, qc=qc: e.tensor_copy(
                QS[64:96, :], AR1[96:128, qc:qc + 2048]), inc=(sV, 1))
            P.emit("vector", lambda e, qc=qc: e.tensor_copy(
                QS[96:128, :], AR1[64:96, qc:qc + 2048]), inc=(sV, 1))
            P.emit("vector", lambda e, qc=qc: e.tensor_tensor(
                AR2[:, qc:qc + 2048], AR1[:, qc:qc + 2048], COSP[:], OP.mult),
                inc=(sV, 1))
            P.emit("vector", lambda e: e.tensor_tensor(QS[:], QS[:], SINP[:],
                                                       OP.mult), inc=(sV, 1))
            P.emit("vector", lambda e, qc=qc: e.tensor_tensor(
                AR2[:, qc:qc + 2048], AR2[:, qc:qc + 2048], QS[:], OP.add),
                inc=(sV, 1))
            q_done_v.append(P.c(sV))
        v_rope = P.c(sV)

        sc_bank_free = {0: 0, 2: 0}
        ov_bank_free = 0
        bc_bank_free = 0
        eslot_free = {0: 0, 1: 0, 2: 0}
        nr_free = 0
        un_free = 0
        pending_norm = None
        aot_last_v = 0
        ow_loaded_g = []
        for hp in range(NF):
            g = hp // 4
            qcol = hp * 2048
            for qb in range(4):
                kmax = 4 * qb + 4
                exp_cnt = {}
                diag_cnt = {}

                def emit_av(tp, last):
                    cp = tp - 4 * qb
                    qoffp = max(cp, 0) * 128
                    wV = [(sC, exp_cnt[tp])]
                    if tp == 0:
                        wV.append((sV, ov_bank_free))
                    P.emit("tensor", lambda e, g=g, tp=tp, qoffp=qoffp: e.matmul(
                        bank(4)[0:65, qoffp:512],
                        VT[:, (g * 16 + tp) * 65:(g * 16 + tp) * 65 + 65],
                        EB[:, (tp % 3) * 1024 + qoffp:(tp % 3) * 1024 + 512],
                        start=(tp == 0), stop=last),
                        waits=wV, inc=(sT, 1))
                    P.emit("tensor", lambda e, g=g, tp=tp, qoffp=qoffp: e.matmul(
                        bank(5)[0:65, qoffp:512],
                        VT[:, (g * 16 + tp) * 65:(g * 16 + tp) * 65 + 65],
                        EB[:, (tp % 3) * 1024 + 512 + qoffp:(tp % 3) * 1024 + 1024],
                        start=(tp == 0), stop=last),
                        inc=(sT, 1))
                    eslot_free[tp % 3] = P.c(sT)

                for t in range(kmax):
                    c = t - 4 * qb
                    qoff = max(c, 0) * 128
                    sb_ = (t % 2) * 2
                    wA = [(sC, sc_bank_free[sb_])]
                    if qb == 0 and t == 0:
                        wA.append((sV, q_done_v[hp]))
                    if c >= 0:
                        P.emit("tensor", lambda e, sb_=sb_, qoff=qoff: e.matmul(
                            bank(sb_)[:, qoff:qoff + 128], DGN[:], IDT[:],
                            start=True, stop=False), waits=wA, inc=(sT, 1))
                        P.emit("tensor", lambda e, sb_=sb_, qoff=qoff: e.matmul(
                            bank(sb_ + 1)[:, qoff:qoff + 128], DGN[:], IDT[:],
                            start=True, stop=False), inc=(sT, 1))
                        wA = []
                    P.emit("tensor", lambda e, g=g, t=t, sb_=sb_, qcol=qcol, qb=qb,
                           qoff=qoff, c=c: e.matmul(
                               bank(sb_)[:, qoff:512],
                               K2[0:64, g * 2048 + t * 128:g * 2048 + (t + 1) * 128],
                               AR2[0:64, qcol + qb * 512 + qoff:qcol + (qb + 1) * 512],
                               start=(c < 0), stop=True),
                           waits=wA, inc=(sT, 1))
                    P.emit("tensor", lambda e, g=g, t=t, sb_=sb_, qcol=qcol, qb=qb,
                           qoff=qoff, c=c: e.matmul(
                               bank(sb_ + 1)[:, qoff:512],
                               K2[64:128, g * 2048 + t * 128:g * 2048 + (t + 1) * 128],
                               AR2[64:128, qcol + qb * 512 + qoff:qcol + (qb + 1) * 512],
                               start=(c < 0), stop=True),
                           inc=(sT, 1))
                    mm_t = P.c(sT)
                    if t >= 2:
                        emit_av(t - 2, last=False)
                    wE = [(sT, max(mm_t, eslot_free[(t % 3)]))]
                    P.emit("scalar", lambda e, sb_=sb_, qoff=qoff, t=t: e.activation(
                        EB[:, (t % 3) * 1024:(t % 3 + 1) * 1024].rearrange(
                            "p (b n) -> p b n", n=512)[:, :, qoff:512],
                        PS[:, sb_ * 512:(sb_ + 2) * 512].rearrange(
                            "p (b n) -> p b n", n=512)[:, :, qoff:512],
                        ACT.Exp, bias=NEGB[:, 0:1], scale=QSCALE),
                        waits=wE, inc=(sC, 1))
                    exp_cnt[t] = P.c(sC)
                    sc_bank_free[sb_] = exp_cnt[t]
                    if t == 1 and pending_norm is not None:
                        pending_norm()
                        pending_norm = None
                    if t == 2:
                        it = hp * 4 + qb
                        if it == 0:
                            for ci in (0, 1):
                                if wq_chunks[ci][0]:
                                    wq_chunks[ci][0](v_rope)
                        if it < len(wq_chunks):
                            wq_chunks[it][1](v_rope)
                            if it + 2 < len(wq_chunks) and wq_chunks[it + 2][0]:
                                wq_chunks[it + 2][0](v_rope)
                emit_av(kmax - 2, last=False)
                emit_av(kmax - 1, last=True)
                t_av = P.c(sT)

                def make_norm(qcol=qcol, qb=qb, t_av=t_av):
                    def norm():
                        nonlocal nr_free, ov_bank_free, bc_bank_free, un_free
                        nonlocal aot_last_v
                        P.emit("vector", lambda e: e.tensor_copy(
                            NR[0:1, 0:1024], PS[64:65, 2048:3072]),
                            waits=[(sT, t_av), (sC, nr_free)], inc=(sV, 1))
                        v_ncp = P.c(sV)
                        P.emit("vector", lambda e: e.tensor_copy(
                            UNB[0:64, 0:512], bank(4)[0:64, :]),
                            waits=[(sV, un_free)], inc=(sV, 1))
                        P.emit("vector", lambda e: e.tensor_copy(
                            UNB[0:64, 512:1024], bank(5)[0:64, :]),
                            inc=(sV, 1))
                        ov_bank_free = P.c(sV)
                        P.emit("scalar", lambda e: e.activation(
                            NR[0:1, 0:1024], NR[0:1, 0:1024], ACT.Ln),
                            waits=[(sV, v_ncp)], inc=(sC, 1))
                        c_ln = P.c(sC)
                        nr_free = c_ln
                        P.emit("scalar", lambda e: e.activation(
                            NR[0:1, 0:1024], NR[0:1, 0:1024], ACT.Exp,
                            scale=-1.0), waits=[(sC, c_ln)], inc=(sC, 1))
                        c_nr = P.c(sC)
                        P.emit("tensor", lambda e: e.matmul(
                            bank(6)[0:64, :], ONE[0:1, 0:64], NR[0:1, 0:512],
                            start=True, stop=True),
                            waits=[(sC, c_nr), (sV, bc_bank_free)], inc=(sT, 1))
                        P.emit("tensor", lambda e: e.matmul(
                            bank(6)[64:128, :], ONE[0:1, 0:64], NR[0:1, 512:1024],
                            start=True, stop=True), inc=(sT, 1))
                        t_bc = P.c(sT)
                        P.emit("vector", lambda e: e.tensor_copy(
                            UNB[0:64, 1024:1536], bank(6)[0:64, :]),
                            waits=[(sT, t_bc)], inc=(sV, 1))
                        P.emit("vector", lambda e: e.tensor_copy(
                            UNB[0:64, 1536:2048], bank(6)[64:128, :]),
                            inc=(sV, 1))
                        bc_bank_free = P.c(sV)
                        P.emit("vector", lambda e: e.tensor_tensor(
                            AR1[0:64, qcol + qb * 512:qcol + (qb + 1) * 512],
                            UNB[0:64, 0:512], UNB[0:64, 1024:1536], OP.mult),
                            inc=(sV, 1))
                        P.emit("vector", lambda e: e.tensor_tensor(
                            AR1[64:128, qcol + qb * 512:qcol + (qb + 1) * 512],
                            UNB[0:64, 512:1024], UNB[0:64, 1536:2048], OP.mult),
                            inc=(sV, 1))
                        un_free = P.c(sV)
                        aot_last_v = P.c(sV)
                    return norm

                pending_norm = make_norm()
            last_score_mm = P.c(sT)
            P.emit("gpsimd", lambda e, hp=hp: e.dma_start(
                out=AR2[:, hp * 2048:(hp + 1) * 2048],
                in_=ow_d[hp * 128:(hp + 1) * 128, :]),
                waits=[(sT, last_score_mm)], inc=(sG, 16))
            ow_loaded_g.append(P.c(sG))
            if hp == NF - 1 and pending_norm is not None:
                pending_norm()
                pending_norm = None
        st_kq = P.c(sS)
        v_p2 = P.c(sV)

        t_p2_end = P.c(sT)
        P.emit("sync", lambda e: e.dma_start(out=K2F[0:1, 2048:4096], in_=bqr_s[:]),
               waits=[(sT, t_p2_end)], inc=(sL, 16))
        ld_bqrow = P.c(sL)
        bq_ev = 0
        for c in range(4):
            bk = c % 2
            P.emit("tensor", lambda e, c=c, bk=bk: e.matmul(
                bank(bk)[0:128, :], ONE[0:1, :],
                K2F[0:1, 2048 + c * 512:2048 + (c + 1) * 512],
                start=True, stop=True),
                waits=[(sL, ld_bqrow), (sT, t_p2_end), (sV, bq_ev)], inc=(sT, 1))
            t_one = P.c(sT)
            P.emit("vector", lambda e, c=c, bk=bk: e.tensor_copy(
                BQ[:, c * 512:(c + 1) * 512], bank(bk)),
                waits=[(sT, t_one), (sV, v_p2)], inc=(sV, 1))
            bq_ev = P.c(sV)
        v_bq_tile = P.c(sV)

        po_bank_free = [0, 0, 0, 0]
        tr_bank_free = [0, 0]
        x1_store = 0
        xqt_store = 0
        x1tr_t = 0
        pending_tr = None
        Y3 = AR3[:, 4096:6144]

        def emit_tr(jd, v_q1d):
            nonlocal_ = None
            for half in range(2):
                pb = bankb(6 + half)
                w = [(sV, v_q1d), (sV, tr_bank_free[half])]
                for f8 in range(8):
                    f = half * 8 + f8
                    P.emit("tensor", lambda e, pb=pb, f8=f8, f=f: e.transpose(
                        pb[:, f8 * 128:(f8 + 1) * 128],
                        AR4[:, f * 128:(f + 1) * 128], IDT[:]),
                        waits=w if f8 == 0 else (),
                        inc=(sT, 1) if f8 == 7 else None)
                ttr = P.c(sT)
                P.emit("vector", lambda e, pb=pb, half=half: e.tensor_copy(
                    AR4[:, 2048 + half * 1024:2048 + (half + 1) * 1024], pb[:]),
                    waits=[(sT, ttr), (sS, xqt_store)], inc=(sV, 1))
                tr_bank_free[half] = P.c(sV)
            v_tr = P.c(sV)
            P.emit("sync", lambda e, jd=jd: e.dma_start(
                out=xqt_s[jd * 128:(jd + 1) * 128, :], in_=AR4[:, 2048:4096]),
                waits=[(sV, v_tr)], inc=(sS, 16))
            return P.c(sT), P.c(sS)

        for j in range(NT):
            b = j % 2
            xres = AR3[:, b * 2048:(b + 1) * 2048]
            P.emit("gpsimd", lambda e, j=j, xres=xres: e.dma_start(
                out=xres[:], in_=x_d[j * 128:(j + 1) * 128, :]),
                waits=[(sV, v_p2 if j < 2 else po_bank_free[3])], inc=(sG, 16))
            w_xres = P.c(sG)
            for cc in range(4):
                wmm = [(sV, po_bank_free[cc]), (sV, aot_last_v),
                       (sG, ow_loaded_g[-1])]
                for k in range(NF):
                    P.emit("tensor", lambda e, j=j, cc=cc, k=k: e.matmul(
                        bank(2 + cc),
                        AR1[:, k * 2048 + j * 128:k * 2048 + (j + 1) * 128],
                        AR2[:, k * 2048 + cc * 512:k * 2048 + (cc + 1) * 512],
                        start=(k == 0), stop=(k == NF - 1)),
                        waits=wmm if k == 0 else (),
                        inc=(sT, 1) if k == NF - 1 else None)
            t_po = P.c(sT)
            if pending_tr is not None:
                x1tr_t, xqt_store = emit_tr(*pending_tr)
                pending_tr = None
            for cc in range(4):
                P.emit("vector", lambda e, cc=cc, xres=xres: e.tensor_tensor(
                    Y3[:, cc * 512:(cc + 1) * 512], bank(2 + cc),
                    xres[:, cc * 512:(cc + 1) * 512], OP.add),
                    waits=[(sT, t_po), (sS, x1_store), (sG, w_xres),
                           (sV, v_p2 if j < 2 else 0)],
                    inc=(sV, 1))
                po_bank_free[cc] = P.c(sV)
            v_y = P.c(sV)
            P.emit("scalar", lambda e: e.activation(AR4[:, 2048:4096], Y3[:],
                                                    ACT.Square,
                                                    accum_out=STT[:, 1:2]),
                   waits=[(sV, v_y), (sS, xqt_store)], inc=(sC, 1))
            c_sq = P.c(sC)
            P.emit("vector", lambda e: e.reduce_sum(STT[:, 0:1], Y3[:], axis=AX.X),
                   inc=(sV, 1))
            vv(lambda e: e.tensor_scalar(STT[:, 2:3], STT[:, 0:1],
                                         1.0 / D, None, OP.mult))
            vv(lambda e: e.tensor_tensor(STT[:, 4:5], STT[:, 2:3],
                                         STT[:, 2:3], OP.mult))
            vv(lambda e: e.scalar_tensor_tensor(STT[:, 3:4], STT[:, 1:2],
                                                1.0 / D, STT[:, 4:5],
                                                OP.mult, OP.subtract),
               extra=[(sC, c_sq)])
            v_var = P.c(sV)
            P.emit("scalar", lambda e: e.activation(STT[:, 9:10], STT[:, 3:4],
                                                    ACT.Ln, bias=STT[:, 14:15]),
                   waits=[(sV, v_var)], inc=(sC, 1))
            c_l1 = P.c(sC)
            P.emit("scalar", lambda e: e.activation(STT[:, 4:5], STT[:, 9:10],
                                                    ACT.Exp, scale=-0.5),
                   waits=[(sC, c_l1)], inc=(sC, 1))
            c_sd = P.c(sC)
            vv(lambda e: e.tensor_scalar(Y3[:], Y3[:], STT[:, 2:3],
                                         STT[:, 4:5], OP.subtract, OP.mult),
               extra=[(sC, c_sd)])
            v_x1 = P.c(sV)
            P.emit("sync", lambda e, j=j: e.dma_start(
                out=x1_s[j * 128:(j + 1) * 128, :], in_=Y3[:]),
                waits=[(sV, v_x1)], inc=(sS, 16))
            x1_store = P.c(sS)
            vv(lambda e: e.reduce_max(STT[:, 5:6], Y3[:], axis=AX.X,
                                      apply_absolute_value=True))
            vv(lambda e: e.tensor_scalar(STT[:, 5:6], STT[:, 5:6],
                                         TINY, None, OP.max))
            vv(lambda e: e.reciprocal(STT[:, 6:7], STT[:, 5:6]))
            vv(lambda e, j=j: e.tensor_scalar(
                S1B[:, j:j + 1], STT[:, 5:6], 1.0 / 127.0, None, OP.mult))
            vv(lambda e: e.tensor_scalar(AR3i[:, 6144:8192], Y3[:],
                                         STT[:, 6:7], 127.0, OP.mult, OP.mult),
               extra=[(sS, st_kq if j == 0 else 0)])
            P.emit("vector", lambda e: e.tensor_copy(AR4[:, 0:2048],
                                                     AR3i[:, 6144:8192]),
                   waits=[(sT, x1tr_t)], inc=(sV, 1))
            v_q1 = P.c(sV)
            pending_tr = (j, v_q1)
        x1tr_t, xqt_store = emit_tr(*pending_tr)
        st_p3 = P.c(sS)
        t_p3 = P.c(sT)

        for k in range(NF):
            P.emit("sync", lambda e, k=k: e.dma_start(
                out=AR2[:, k * 2048:(k + 1) * 2048],
                in_=kq_s[k * 128:(k + 1) * 128, :]),
                waits=[(sT, t_p3), (sS, st_p3)] if k == 0 else (), inc=(sL, 16))
        ld_kq = P.c(sL)
        pf_bank_free = [0, 0, 0, 0]
        pf_last_mm = [t_p3, t_p3]
        y2_done = [0, 0]
        yslot_free = [0, 0]
        ebi_free = 0
        for j in range(NT):
            b = j % 2
            xqst = AR4[:, b * 2048:(b + 1) * 2048]
            x1st = AR3[:, b * 2048:(b + 1) * 2048]
            P.emit("sync", lambda e, j=j, xqst=xqst: e.dma_start(
                out=xqst[:], in_=xqt_s[j * 128:(j + 1) * 128, :]),
                waits=([(sT, pf_last_mm[b])] if j >= 2 else
                       [(sT, t_p3), (sS, st_p3)]), inc=(sL, 16))
            P.emit("sync", lambda e, j=j, x1st=x1st: e.dma_start(
                out=x1st[:], in_=x1_s[j * 128:(j + 1) * 128, :]),
                waits=([(sV, y2_done[b])] if j >= 2 else [(sS, st_p3)]),
                inc=(sL, 16))
            w_ld = P.c(sL)
            for cc in range(4):
                wmm = [(sV, pf_bank_free[cc]), (sL, w_ld), (sL, ld_kq)]
                for k in range(NF):
                    P.emit("tensor", lambda e, cc=cc, k=k, xqst=xqst: e.matmul(
                        bank(2 + cc), xqst[:, k * 128:(k + 1) * 128],
                        AR2[:, k * 2048 + cc * 512:k * 2048 + (cc + 1) * 512],
                        start=(k == 0), stop=(k == NF - 1)),
                        waits=wmm if (k == 0 and cc == 0) else (),
                        inc=(sT, 1) if k == NF - 1 else None)
            pf_last_mm[b] = P.c(sT)
            t_pf = P.c(sT)
            Y = AR3[:, 4096 + b * 2048:4096 + (b + 1) * 2048]
            for cc in range(4):
                P.emit("vector", lambda e, cc=cc, j=j, Y=Y: e.scalar_tensor_tensor(
                    Y[:, cc * 512:(cc + 1) * 512], bank(2 + cc), S1B[:, j:j + 1],
                    BQ[:, cc * 512:(cc + 1) * 512], OP.mult, OP.add),
                    waits=[(sT, t_pf), (sS, yslot_free[b])], inc=(sV, 1))
                pf_bank_free[cc] = P.c(sV)
            P.emit("vector", lambda e, Y=Y: e.tensor_scalar(Y[:], Y[:], 0.0, None,
                                                            OP.max), inc=(sV, 1))
            vv(lambda e, Y=Y: e.reduce_max(
                STT[:, 5:6], Y[:], axis=AX.X, apply_absolute_value=True))
            vv(lambda e: e.tensor_scalar(STT[:, 5:6], STT[:, 5:6],
                                         TINY, None, OP.max))
            vv(lambda e: e.reciprocal(STT[:, 6:7], STT[:, 5:6]))
            vv(lambda e: e.tensor_scalar(STT[:, 7:8], STT[:, 5:6],
                                         1.0 / 127.0, None, OP.mult))
            for hh in range(2):
                vv(lambda e, hh=hh, Y=Y: e.tensor_scalar(
                    EBi[:, 0:1024], Y[:, hh * 1024:(hh + 1) * 1024], STT[:, 6:7],
                    127.0, OP.mult, OP.mult), extra=[(sC, ebi_free)])
                P.emit("vector", lambda e, hh=hh, Y=Y, x1st=x1st:
                       e.scalar_tensor_tensor(
                           Y[:, hh * 1024:(hh + 1) * 1024], EBi[:, 0:1024],
                           STT[:, 7:8], x1st[:, hh * 1024:(hh + 1) * 1024],
                           OP.mult, OP.add), inc=(sV, 1))
            v_y2 = P.c(sV)
            y2_done[b] = v_y2
            P.emit("scalar", lambda e, Y=Y: e.activation(EB[:, 0:2048], Y[:],
                                                         ACT.Square,
                                                         accum_out=STT[:, 1:2]),
                   waits=[(sV, v_y2)], inc=(sC, 1))
            c_sq = P.c(sC)
            ebi_free = c_sq
            P.emit("vector", lambda e, Y=Y: e.reduce_sum(STT[:, 0:1], Y[:],
                                                         axis=AX.X), inc=(sV, 1))
            vv(lambda e: e.tensor_scalar(STT[:, 2:3], STT[:, 0:1],
                                         1.0 / D, None, OP.mult))
            vv(lambda e: e.tensor_tensor(STT[:, 4:5], STT[:, 2:3],
                                         STT[:, 2:3], OP.mult))
            vv(lambda e: e.scalar_tensor_tensor(STT[:, 3:4], STT[:, 1:2],
                                                1.0 / D, STT[:, 4:5],
                                                OP.mult, OP.subtract),
               extra=[(sC, c_sq)])
            v_var = P.c(sV)
            P.emit("scalar", lambda e: e.activation(STT[:, 9:10], STT[:, 3:4],
                                                    ACT.Ln, bias=STT[:, 14:15]),
                   waits=[(sV, v_var)], inc=(sC, 1))
            c_l1 = P.c(sC)
            P.emit("scalar", lambda e: e.activation(STT[:, 4:5], STT[:, 9:10],
                                                    ACT.Exp, scale=-0.5),
                   waits=[(sC, c_l1)], inc=(sC, 1))
            c_sd = P.c(sC)
            vv(lambda e, Y=Y: e.tensor_scalar(
                Y[:], Y[:], STT[:, 2:3], STT[:, 4:5], OP.subtract, OP.mult),
               extra=[(sC, c_sd)])
            v_out = P.c(sV)
            P.emit("gpsimd", lambda e, j=j, Y=Y: e.dma_start(
                out=out_d[j * 128:(j + 1) * 128, :], in_=Y[:]),
                waits=[(sV, v_out)], inc=(sS, 16))
            yslot_free[b] = P.c(sS)
        st_total = P.c(sS)
        P.emit("sync", lambda e: e.nop(), waits=[(sS, st_total)])

        @blk.sync
        def _(e):
            for op in P.ops["sync"]:
                op(e)

        @blk.gpsimd
        def _(e):
            for op in P.ops["gpsimd"]:
                op(e)

        @blk.vector
        def _(e):
            for op in P.ops["vector"]:
                op(e)

        @blk.scalar
        def _(e):
            for op in P.ops["scalar"]:
                op(e)

        @blk.tensor
        def _(e):
            for op in P.ops["tensor"]:
                op(e)

    return nc


_NC_CACHE = None


def _consts():
    ident = np.eye(128, dtype=ml_dtypes.bfloat16)
    diag01 = np.triu(np.ones((128, 128), dtype=np.float32)).astype(ml_dtypes.bfloat16)
    kk = np.arange(128)[:, None]
    qq = np.arange(128)[None, :]
    mneg = np.where(kk > qq, np.float32(-1e30), np.float32(0.0))
    diagneg = np.ascontiguousarray(mneg.T).astype(ml_dtypes.bfloat16)
    inv = 1.0 / (10000.0 ** (np.arange(0, 64, 2, dtype=np.float64) / 64.0))
    ang = np.arange(2048, dtype=np.float64)[None, :] * inv[:, None]
    cosf, sinf = np.cos(ang), np.sin(ang)
    cosp = np.ascontiguousarray(np.tile(cosf, (4, 1))).astype(ml_dtypes.bfloat16)
    sinp = np.ascontiguousarray(
        np.concatenate([-sinf, sinf, -sinf, sinf], 0)).astype(ml_dtypes.bfloat16)
    return ident, diag01, diagneg, cosp, sinp


def kernel(L0_attn_hidden_states, L0_attn_kv_a_w, L0_attn_kv_b_w, L0_attn_o_w,
           L0_ffn_kernel, L0_ffn_bias):
    global _NC_CACHE
    hs = np.ascontiguousarray(np.asarray(L0_attn_hidden_states))
    kva = np.ascontiguousarray(np.asarray(L0_attn_kv_a_w))
    kvb = np.ascontiguousarray(np.asarray(L0_attn_kv_b_w))
    ow = np.ascontiguousarray(np.asarray(L0_attn_o_w))
    fk = np.ascontiguousarray(np.asarray(L0_ffn_kernel))
    fb = np.ascontiguousarray(np.asarray(L0_ffn_bias)).reshape(1, -1)
    ident, diag01, diagneg, cosp, sinp = _consts()
    if _NC_CACHE is None:
        _NC_CACHE = build_nc()
    in_maps = []
    for c in range(8):
        in_maps.append({
            "x": hs[c], "kv_a_w": kva, "kv_b_w": kvb, "o_w": ow,
            "fk": fk, "fb": fb, "ident": ident, "diag01": diag01,
            "diagneg": diagneg, "cosp": cosp, "sinp": sinp,
        })
    res = run_bass_kernel_spmd(_NC_CACHE, in_maps, core_ids=list(range(8)))
    return np.stack([np.asarray(res.results[c]["out"]) for c in range(8)], axis=0)
